# version 01
# baseline (speedup 1.0000x reference)
import math
import numpy as np
import concourse.bass as bass
import concourse.mybir as mybir
from concourse.bass_utils import run_bass_kernel_spmd

F32 = mybir.dt.float32
BF16 = mybir.dt.bfloat16
U8 = mybir.dt.uint8
AF = mybir.ActivationFunctionType
ALU = mybir.AluOpType
AX = mybir.AxisListType

N_CORES = 8
D = 1024
BATCH = 2
SEQ = 8192
T_CORE = BATCH * SEQ // N_CORES
NT = D // 128
TCH = 512
N_TCH = T_CORE // TCH



def pack_lhsT(W, tile_o=128):
    in_dim, out_dim = W.shape
    assert in_dim % 128 == 0 and out_dim % tile_o == 0
    n_ic = in_dim // 128
    n_oc = out_dim // tile_o
    Wr = W.reshape(n_ic, 128, n_oc, tile_o).transpose(2, 1, 0, 3)
    return np.ascontiguousarray(Wr.reshape(n_oc, 128, n_ic * tile_o))


def shard_tokens(x):
    xt = np.ascontiguousarray(x.reshape(BATCH * SEQ, x.shape[-1]).T)
    return [np.ascontiguousarray(xt[:, i * T_CORE:(i + 1) * T_CORE])
            for i in range(N_CORES)]


def unshard_tokens(shards):
    xt = np.concatenate(shards, axis=1)
    return np.ascontiguousarray(xt.T).reshape(BATCH, SEQ, xt.shape[0])



class Op:
    __slots__ = ("eng", "fn", "deps", "need_inc", "inc", "tag", "sem_key",
                 "wait_val")

    def __init__(self, eng, fn, inc, tag=""):
        self.eng = eng
        self.fn = fn
        self.deps = []
        self.need_inc = False
        self.inc = inc
        self.tag = tag
        self.sem_key = eng
        self.wait_val = None


class Buf:
    __slots__ = ("name", "ap", "last_write", "readers", "is_psum")

    def __init__(self, name, ap, is_psum=False):
        self.name = name
        self.ap = ap
        self.last_write = None
        self.readers = []
        self.is_psum = is_psum

    def __getitem__(self, idx):
        return self.ap[idx]


PHYS = {"T": "tensor", "V": "vector", "S": "scalar", "G": "gpsimd",
        "QS": "sync", "QG": "gpsimd", "QA": "scalar"}


class KB:
    DMA_SLOTS = 8

    def __init__(self, name="mk"):
        self.nc = bass.Bass("TRN2", target_bir_lowering=False, debug=False)
        self.ops = {e: [] for e in PHYS}
        self._all = []
        self.consts = {}
        self._nbuf = 0
        self._qcount = {"QS": 0, "QG": 0, "QA": 0}
        self._qslot_last = {}

    def dram_in(self, name, shape, dtype=F32):
        return Buf(name, self.nc.dram_tensor(name, list(shape), dtype,
                                             kind="ExternalInput").ap())

    def dram_out(self, name, shape, dtype=F32):
        return Buf(name, self.nc.dram_tensor(name, list(shape), dtype,
                                             kind="ExternalOutput").ap())

    def dram_tmp(self, name, shape, dtype=F32):
        return Buf(name, self.nc.dram_tensor(name, list(shape), dtype,
                                             kind="Internal").ap())

    def sbuf(self, name, shape, dtype=F32):
        self._nbuf += 1
        return Buf(name, self.nc.alloc_sbuf_tensor(
            f"{name}_{self._nbuf}", list(shape), dtype).ap())

    def psum(self, name, shape, dtype=F32):
        self._nbuf += 1
        return Buf(name, self.nc.alloc_psum_tensor(
            f"{name}_{self._nbuf}", list(shape), dtype).ap(), is_psum=True)

    def const(self, val, dtype=F32):
        key = (float(val), dtype)
        if key not in self.consts:
            b = self.sbuf("cst", (128, 1), dtype)
            try:
                self.nc.const_aps.register(dtype, float(val), b.ap)
            except Exception:
                pass
            self.op("G", (lambda g, a=b.ap, v=float(val): g.memset(a, v)),
                    writes=[b], tag=f"const{val}")
            self.consts[key] = b
        return self.consts[key]

    def op(self, eng, fn, reads=(), writes=(), tag=""):
        o = Op(eng, fn, 16 if eng in ("QS", "QG", "QA") else 1, tag=tag)
        skipT = eng == "T"

        def _dep(d):
            if d is None or d is o:
                return
            if skipT and d.eng == "T":
                return
            o.deps.append(d)

        for b in reads:
            _dep(b.last_write)
            if b.is_psum:
                for r in b.readers:
                    if r.eng != eng:
                        _dep(r)
            b.readers.append(o)
        for b in writes:
            _dep(b.last_write)
            for r in b.readers:
                _dep(r)
            b.last_write = o
            b.readers = []
        if eng in ("QS", "QG", "QA"):
            i = self._qcount[eng]
            self._qcount[eng] = i + 1
            slot = (eng, i % self.DMA_SLOTS)
            o.sem_key = slot
            o.wait_val = 16 * (i // self.DMA_SLOTS + 1)
            prev = self._qslot_last.get(slot)
            if prev is not None:
                o.deps.append(prev)
            self._qslot_last[slot] = o
        self.ops[eng].append(o)
        self._all.append((eng, o))
        return o

    def dma(self, q, dst, dst_ap, src, src_ap, tag=""):
        return self.op(q, (lambda e, a=dst_ap, b=src_ap: e.dma_start(a, b)),
                       reads=([src] if src is not None else []),
                       writes=([dst] if dst is not None else []), tag=tag)

    def mm(self, psum, psum_ap, lhsT, lhsT_ap, rhs, rhs_ap, start, stop, tag=""):
        return self.op(
            "T",
            (lambda t, o=psum_ap, l=lhsT_ap, r=rhs_ap, s1=start, s2=stop:
             t.matmul(o, l, r, start=s1, stop=s2)),
            reads=[lhsT, rhs], writes=[psum], tag=tag)

    def act(self, eng, out, out_ap, in_, in_ap, func, bias=None, scale=1.0,
            extra_reads=(), tag=""):
        reads = [in_] + list(extra_reads)
        if isinstance(bias, tuple):
            reads.append(bias[0]); b_ap = bias[1]
        elif isinstance(bias, Buf):
            reads.append(bias); b_ap = bias.ap
        else:
            b_ap = bias
        kw = {}
        if b_ap is not None:
            kw["bias"] = b_ap
        return self.op(eng, (lambda s, o=out_ap, i=in_ap, f=func, sc=scale,
                             kw=kw: s.activation(o, i, f, scale=sc, **kw)),
                       reads=reads, writes=[out], tag=tag)

    def tt(self, eng, out, out_ap, a, a_ap, b, b_ap, alu, tag=""):
        return self.op(eng, (lambda v, o=out_ap, x=a_ap, y=b_ap, op=alu:
                             v.tensor_tensor(o, x, y, op)),
                       reads=[a, b], writes=[out], tag=tag)

    def ts(self, eng, out, out_ap, a, a_ap, s1, op0, s2=None, op1=ALU.bypass,
           tag=""):
        def _sc(s, reads):
            if isinstance(s, tuple):
                reads.append(s[0]); return s[1]
            if isinstance(s, Buf):
                reads.append(s); return s.ap
            return s
        reads = [a]
        s1_ap = _sc(s1, reads)
        s2_ap = _sc(s2, reads)
        return self.op(eng, (lambda v, o=out_ap, x=a_ap, u=s1_ap, w=s2_ap,
                             p0=op0, p1=op1:
                             v.tensor_scalar(o, x, u, w, p0, p1)),
                       reads=reads, writes=[out], tag=tag)

    def stt(self, eng, out, out_ap, a, a_ap, s1, b, b_ap, op0, op1, tag=""):
        reads = [a, b]
        if isinstance(s1, tuple):
            reads.append(s1[0]); s1_ap = s1[1]
        elif isinstance(s1, Buf):
            reads.append(s1); s1_ap = s1.ap
        else:
            s1_ap = s1
        return self.op(eng, (lambda v, o=out_ap, x=a_ap, u=s1_ap, y=b_ap,
                             p0=op0, p1=op1:
                             v.scalar_tensor_tensor(o, x, u, y, p0, p1)),
                       reads=reads, writes=[out], tag=tag)

    def copy(self, eng, out, out_ap, in_, in_ap, tag=""):
        if eng == "S":
            return self.op(eng, (lambda s, o=out_ap, i=in_ap:
                                 s.activation(o, i, AF.Copy)),
                           reads=[in_], writes=[out], tag=tag)
        return self.op(eng, (lambda v, o=out_ap, i=in_ap: v.tensor_copy(o, i)),
                       reads=[in_], writes=[out], tag=tag)

    def reduce(self, eng, out, out_ap, in_, in_ap, alu=ALU.add, negate=None,
               tag=""):
        return self.op(eng, (lambda v, o=out_ap, i=in_ap, op=alu, n=negate:
                             v.tensor_reduce(o, i, AX.X, op, negate=n)),
                       reads=[in_], writes=[out], tag=tag)

    def memset(self, eng, buf, ap, val, tag=""):
        return self.op(eng, (lambda v, a=ap, x=float(val): v.memset(a, x)),
                       writes=[buf], tag=tag)

    def build(self):
        import os
        nc = self.nc
        mode = os.environ.get("KB_SERIALIZE", "")
        if mode:
            def _cat(e):
                if e == "T":
                    return "T"
                if e in ("QS", "QG", "QA"):
                    return "Q"
                if e == "G":
                    return "G"
                return "VS"
            prev = None
            for e, o in self._all:
                if prev is not None and prev.eng != o.eng:
                    cats = {_cat(prev.eng), _cat(o.eng)}
                    add = (mode == "all" or mode in cats
                           or (mode == "Qin" and _cat(prev.eng) == "Q")
                           or (mode == "Qout" and _cat(o.eng) == "Q"))
                    if add:
                        o.deps.append(prev)
                prev = o
        for e, ops in self.ops.items():
            for o in ops:
                for d in o.deps:
                    d.need_inc = True
        for q in ("QS", "QG", "QA"):
            for o in self.ops[q]:
                o.need_inc = True

        sem_keys = set(PHYS)
        for e, o in self._all:
            sem_keys.add(o.sem_key)
        sems = {k: nc.alloc_semaphore(f"sem_{k}") for k in sem_keys}
        val = {k: 0 for k in sem_keys}
        opval = {}
        for e, o in self._all:
            if o.need_inc:
                if o.wait_val is not None:
                    opval[id(o)] = o.wait_val
                    val[o.sem_key] = o.wait_val
                else:
                    val[o.sem_key] += o.inc
                    opval[id(o)] = val[o.sem_key]
        self.final_counts = dict(val)

        phys_ops = {p: [] for p in ("tensor", "vector", "scalar", "gpsimd",
                                    "sync")}
        for e, o in self._all:
            phys_ops[PHYS[e]].append(o)

        def emit_stream(handle, items):
            watermark = {}
            for o in items:
                needs = {}
                for d in o.deps:
                    v = opval.get(id(d))
                    if v is None:
                        continue
                    k = d.sem_key
                    if needs.get(k, 0) < v:
                        needs[k] = v
                for de in sorted(needs, key=str):
                    v = needs[de]
                    if watermark.get(de, 0) >= v:
                        continue
                    handle.wait_ge(sems[de], v)
                    watermark[de] = v
                inst = o.fn(handle)
                if o.need_inc:
                    inst.then_inc(sems[o.sem_key], o.inc)

        with nc.Block() as blk:
            @blk.tensor
            def _(t):
                emit_stream(t, phys_ops["tensor"])

            @blk.vector
            def _(v):
                emit_stream(v, phys_ops["vector"])

            @blk.scalar
            def _(s):
                emit_stream(s, phys_ops["scalar"])

            @blk.gpsimd
            def _(g):
                emit_stream(g, phys_ops["gpsimd"])

            @blk.sync
            def _(s):
                emit_stream(s, phys_ops["sync"])
                for k, v in sorted(self.final_counts.items(), key=lambda x: str(x[0])):
                    if not isinstance(k, tuple):
                        continue
                    if v:
                        s.wait_ge(sems[k], v)
        return nc



class Ctx:

    def __init__(self, kb, T=T_CORE):
        self.kb = kb
        self.T = T
        self.ntch = T // TCH
        self.xn = [kb.sbuf(f"xn{i}", (128, T), BF16) for i in range(NT)]
        self.pool = [kb.psum(f"pp{i}", (128, TCH), F32) for i in range(3)]
        self._pi = 0
        self.stats_ps = kb.psum("stats", (33, TCH), F32)
        self.bc_ps = [kb.psum(f"bc{i}", (128, TCH), F32) for i in range(2)]
        self.tr_ps = kb.psum("tr_ps", (128, TCH), F32)
        self.tr_ps_bf = kb.psum("tr_ps_bf", (128, 2 * TCH), BF16)
        self.rowsA = kb.sbuf("rowsA", (128, T), F32)
        self.rowsB = kb.sbuf("rowsB", (128, T), F32)
        self.xst = [kb.sbuf("xst0", (128, NT * TCH), F32)]
        self.xr = [kb.sbuf(f"xr{i}", (128, TCH), F32) for i in range(2)]
        self.xo = [kb.sbuf(f"xo{i}", (128, TCH), F32) for i in range(2)]
        self.sc = [kb.sbuf("sc0", (128, TCH), F32), None]
        self.ones_col = kb.sbuf("ones_col", (1, 128), F32)
        kb.memset("G", self.ones_col, self.ones_col.ap, 1.0)
        self.ones_col_bf = kb.sbuf("ones_col_bf", (1, 128), BF16)
        kb.memset("G", self.ones_col_bf, self.ones_col_bf.ap, 1.0)
        self.neg_half_row = kb.sbuf("neghalf", (1, 128), F32)
        kb.memset("G", self.neg_half_row, self.neg_half_row.ap, -0.5)
        self.onesA = kb.sbuf("onesA", (128, 33), F32)
        kb.memset("G", self.onesA, self.onesA.ap, 0.0)
        kb.op("G", (lambda g, a=self.onesA.ap: g.memset(a[:, 0:1], 1.0)),
              writes=[self.onesA])
        self.onesB = kb.sbuf("onesB", (128, 33), F32)
        kb.memset("G", self.onesB, self.onesB.ap, 0.0)
        kb.op("G", (lambda g, a=self.onesB.ap: g.memset(a[:, 32:33], 1.0)),
              writes=[self.onesB])
        self.ones128_bf = kb.sbuf("ones128bf", (128, 1), BF16)
        kb.memset("G", self.ones128_bf, self.ones128_bf.ap, 1.0)
        self.onesA_bf = kb.sbuf("onesAbf", (128, 33), BF16)
        kb.memset("G", self.onesA_bf, self.onesA_bf.ap, 0.0)
        kb.op("G", (lambda g, a=self.onesA_bf.ap: g.memset(a[:, 0:1], 1.0)),
              writes=[self.onesA_bf])
        self.onesB_bf = kb.sbuf("onesBbf", (128, 33), BF16)
        kb.memset("G", self.onesB_bf, self.onesB_bf.ap, 0.0)
        kb.op("G", (lambda g, a=self.onesB_bf.ap: g.memset(a[:, 32:33], 1.0)),
              writes=[self.onesB_bf])

    def psum_next(self):
        b = self.pool[self._pi % len(self.pool)]
        self._pi += 1
        return b


def bcast_scalar(kb, ctx, src, src_ap, scale=1.0, name="bs"):
    ps = ctx.bc_ps[0]
    kb.mm(ps, ps.ap[:, 0:1], ctx.ones_col, ctx.ones_col.ap, src, src_ap,
          True, True, tag=name)
    out = kb.sbuf(name, (128, 1), F32)
    kb.ts("V", out, out.ap, ps, ps.ap[:, 0:1], float(scale), ALU.mult, tag=name)
    return out


def emit_ln(kb, ctx, x_dram, eps=1e-5):
    T = ctx.T
    eps_c = kb.const(eps)
    for tc in range(ctx.ntch):
        sl = slice(tc * TCH, (tc + 1) * TCH)
        st = ctx.xst[0]
        for ci in range(NT):
            kb.dma("QS", st, st.ap[:, ci * TCH:(ci + 1) * TCH],
                   x_dram, x_dram.ap[ci * 128:(ci + 1) * 128, sl],
                   tag=f"ln_ld{tc}_{ci}")
        for ci in range(NT):
            x_ap = st.ap[:, ci * TCH:(ci + 1) * TCH]
            sq = ctx.sc[0]
            kb.act("S", sq, sq.ap, st, x_ap, AF.Square, tag=f"ln_sq")
            kb.mm(ctx.stats_ps, ctx.stats_ps.ap, ctx.onesA, ctx.onesA.ap,
                  st, x_ap, ci == 0, False, tag="ln_sum")
            kb.mm(ctx.stats_ps, ctx.stats_ps.ap, ctx.onesB, ctx.onesB.ap,
                  sq, sq.ap, False, ci == NT - 1, tag="ln_sumsq")
        RA, RB = ctx.rowsA, ctx.rowsB
        m_ap = RA.ap[0:1, sl]
        ms_ap = RA.ap[64:65, sl]
        vD_ap = RA.ap[32:33, sl]
        r_ap = RB.ap[0:1, sl]
        kb.ts("V", RA, m_ap, ctx.stats_ps, ctx.stats_ps.ap[0:1, :],
              1.0 / D, ALU.mult, tag="ln_m")
        kb.act("S", RA, ms_ap, RA, m_ap, AF.Square, tag="ln_msq")
        kb.ts("V", RA, ms_ap, RA, ms_ap, -float(D), ALU.mult, tag="ln_msqD")
        kb.tt("V", RA, vD_ap, ctx.stats_ps, ctx.stats_ps.ap[32:33, :],
              RA, ms_ap, ALU.add, tag="ln_var")
        kb.act("S", RA, vD_ap, RA, vD_ap, AF.Sqrt, bias=(eps_c, eps_c.ap[0:1, :]),
               scale=1.0 / D, tag="ln_sqrt")
        kb.op("V", (lambda vec, o=r_ap, i=vD_ap: vec.reciprocal(o, i)),
              reads=[RA], writes=[RB], tag="ln_recip")
        kb.mm(ctx.bc_ps[0], ctx.bc_ps[0].ap, ctx.ones_col, ctx.ones_col.ap,
              RA, m_ap, True, True, tag="ln_bcm")
        kb.mm(ctx.bc_ps[1], ctx.bc_ps[1].ap, ctx.ones_col, ctx.ones_col.ap,
              RB, r_ap, True, True, tag="ln_bcr")
        for ci in range(NT):
            x_ap = st.ap[:, ci * TCH:(ci + 1) * TCH]
            t1 = ctx.sc[0]
            kb.tt("V", t1, t1.ap, st, x_ap, ctx.bc_ps[0], ctx.bc_ps[0].ap,
                  ALU.subtract, tag="ln_sub")
            kb.tt("V", ctx.xn[ci], ctx.xn[ci].ap[:, sl], t1, t1.ap,
                  ctx.bc_ps[1], ctx.bc_ps[1].ap, ALU.mult, tag="ln_apply")


def res_add_store(kb, ctx, psum, psum_ap, scale, x_dram, xout_dram, ci, sl,
                  par, tag=""):
    xr, xo = ctx.xr[par], ctx.xo[par]
    kb.dma("QS", xr, xr.ap, x_dram, x_dram.ap[ci * 128:(ci + 1) * 128, sl],
           tag=tag + "_ldx")
    kb.stt("V", xo, xo.ap, psum, psum_ap, scale, xr, xr.ap, ALU.mult, ALU.add,
           tag=tag + "_add")
    kb.dma("QS", xout_dram, xout_dram.ap[ci * 128:(ci + 1) * 128, sl],
           xo, xo.ap, tag=tag + "_st")



def emit_l0_ffn(kb, ctx, w, x_dram, xout_dram):
    T = ctx.T
    s_sb = {}
    for nm in ("s0", "s1", "s2"):
        b = kb.sbuf(nm + "_sb", (1, 1), F32)
        kb.dma("QS", b, b.ap, w[nm], w[nm].ap, tag=nm)
        s_sb[nm] = b
    sc0 = bcast_scalar(kb, ctx, s_sb["s0"], s_sb["s0"].ap, 2.0, "sc0")
    sc1 = bcast_scalar(kb, ctx, s_sb["s1"], s_sb["s1"].ap, 4.0, "sc1")
    sc2 = bcast_scalar(kb, ctx, s_sb["s2"], s_sb["s2"].ap, 4.0, "sc2")

    w0t = [kb.sbuf(f"w0t{i}", (128, 1024), BF16) for i in range(2)]
    w0b = [kb.sbuf(f"w0b{i}", (128, 1024), BF16) for i in range(2)]
    w0d = [kb.sbuf(f"w0d{i}", (128, 1024), BF16) for i in range(2)]
    w12 = [kb.sbuf(f"w12{i}", (128, 4096), BF16) for i in range(2)]
    e1c = kb.sbuf("e1c", (128, 32 * TCH), BF16)
    e2c = kb.sbuf("e2c", (128, 32 * TCH), BF16)
    ssum1 = kb.sbuf("ssum1", (1, TCH), F32)
    ssum2 = kb.sbuf("ssum2", (1, TCH), F32)

    for tc in range(ctx.ntch):
        sl = slice(tc * TCH, (tc + 1) * TCH)
        for oc in range(32):
            p = oc % 2
            kb.dma("QG", w0t[p], w0t[p].ap, w["k0t"], w["k0t"].ap[oc], tag="l0wt")
            kb.dma("QG", w0b[p], w0b[p].ap, w["k0b"], w["k0b"].ap[oc], tag="l0wb")
            kb.tt("V", w0d[p], w0d[p].ap, w0t[p], w0t[p].ap, w0b[p],
                  w0b[p].ap, ALU.subtract, tag="l0wd")
            ps = ctx.psum_next()
            for ci in range(NT):
                kb.mm(ps, ps.ap, w0d[p], w0d[p].ap[:, ci * 128:(ci + 1) * 128],
                      ctx.xn[ci], ctx.xn[ci].ap[:, sl], ci == 0, ci == NT - 1,
                      tag="l0mm")
            kb.act("S", e1c, e1c.ap[:, oc * TCH:(oc + 1) * TCH], ps, ps.ap,
                   AF.Relu, scale=sc0.ap, extra_reads=[sc0], tag="l0evac")
        for oc in range(32):
            kb.mm(ctx.stats_ps, ctx.stats_ps.ap[0:1, :], ctx.ones128_bf,
                  ctx.ones128_bf.ap, e1c, e1c.ap[:, oc * TCH:(oc + 1) * TCH],
                  oc == 0, oc == 31, tag="l0ssum")
        kb.copy("V", ssum1, ssum1.ap, ctx.stats_ps, ctx.stats_ps.ap[0:1, :],
                tag="l0ssume")
        for oc in range(32):
            p = oc % 2
            kb.dma("QG", w12[p], w12[p].ap, w["k1p"], w["k1p"].ap[oc], tag="l1w")
            ps = ctx.psum_next()
            for fi in range(32):
                kb.mm(ps, ps.ap, w12[p], w12[p].ap[:, fi * 128:(fi + 1) * 128],
                      e1c, e1c.ap[:, fi * TCH:(fi + 1) * TCH], fi == 0, False,
                      tag="l1mm")
            kb.mm(ps, ps.ap, ctx.neg_half_row, ctx.neg_half_row.ap,
                  ssum1, ssum1.ap, False, True, tag="l1row")
            kb.act("S", e2c, e2c.ap[:, oc * TCH:(oc + 1) * TCH], ps, ps.ap,
                   AF.Relu, scale=sc1.ap, extra_reads=[sc1], tag="l1evac")
        for oc in range(32):
            kb.mm(ctx.stats_ps, ctx.stats_ps.ap[0:1, :], ctx.ones128_bf,
                  ctx.ones128_bf.ap, e2c, e2c.ap[:, oc * TCH:(oc + 1) * TCH],
                  oc == 0, oc == 31, tag="l1ssum")
        kb.copy("V", ssum2, ssum2.ap, ctx.stats_ps, ctx.stats_ps.ap[0:1, :],
                tag="l1ssume")
        for oc in range(NT):
            p = oc % 2
            kb.dma("QG", w12[p], w12[p].ap, w["k2p"], w["k2p"].ap[oc], tag="l2w")
            ps = ctx.psum_next()
            for fi in range(32):
                kb.mm(ps, ps.ap, w12[p], w12[p].ap[:, fi * 128:(fi + 1) * 128],
                      e2c, e2c.ap[:, fi * TCH:(fi + 1) * TCH], fi == 0, False,
                      tag="l2mm")
            kb.mm(ps, ps.ap, ctx.neg_half_row, ctx.neg_half_row.ap,
                  ssum2, ssum2.ap, False, True, tag="l2row")
            res_add_store(kb, ctx, ps, ps.ap, sc2, x_dram, xout_dram, oc, sl,
                          oc % 2, tag="l2res")


def l0_ffn_host_inputs(inputs):
    k0 = np.asarray(inputs["L0_ffn_k0"]).view(np.uint8)
    k1 = np.asarray(inputs["L0_ffn_k1"]).view(np.uint8)
    k2 = np.asarray(inputs["L0_ffn_k2"]).view(np.uint8)
    return {
        "k0t": pack_lhsT(k0[:1024]),
        "k0b": pack_lhsT(k0[1024:]),
        "k1p": pack_lhsT(k1),
        "k2p": pack_lhsT(k2),
        "s0": np.asarray(inputs["L0_ffn_s0"], np.float32).reshape(1, 1),
        "s1": np.asarray(inputs["L0_ffn_s1"], np.float32).reshape(1, 1),
        "s2": np.asarray(inputs["L0_ffn_s2"], np.float32).reshape(1, 1),
    }


L0_FFN_WSPEC = {"k0t": ((32, 128, 1024), U8), "k0b": ((32, 128, 1024), U8),
                "k1p": ((32, 128, 4096), U8), "k2p": ((8, 128, 4096), U8),
                "s0": ((1, 1), F32), "s1": ((1, 1), F32), "s2": ((1, 1), F32)}


def build_block_launch(emit_fn, wspec):
    kb = KB()
    x_in = kb.dram_in("x_in", (D, T_CORE), F32)
    x_out = kb.dram_out("x_out", (D, T_CORE), F32)
    w = {nm: kb.dram_in(nm, shape, dt) for nm, (shape, dt) in wspec.items()}
    ctx = Ctx(kb)
    emit_ln(kb, ctx, x_in)
    emit_fn(kb, ctx, w, x_in, x_out)
    nc = kb.build()
    return nc



def emit_l1_ffn(kb, ctx, w, x_dram, xout_dram):
    T = ctx.T
    mur, muk = [], []
    for ci in range(NT):
        a = kb.sbuf(f"mur{ci}", (128, 1), F32)
        kb.dma("QS", a, a.ap, w["mu_r"], w["mu_r"].ap[ci * 128:(ci + 1) * 128, :])
        mur.append(a)
        b = kb.sbuf(f"muk{ci}", (128, 1), F32)
        kb.dma("QS", b, b.ap, w["mu_k"], w["mu_k"].ap[ci * 128:(ci + 1) * 128, :])
        muk.append(b)
    omr, omk = [], []
    for ci in range(NT):
        a = kb.sbuf(f"omr{ci}", (128, 1), F32)
        kb.ts("V", a, a.ap, mur[ci], mur[ci].ap, -1.0, ALU.mult, 1.0, ALU.add)
        omr.append(a)
        b = kb.sbuf(f"omk{ci}", (128, 1), F32)
        kb.ts("V", b, b.ap, muk[ci], muk[ci].ap, -1.0, ALU.mult, 1.0, ALU.add)
        omk.append(b)

    xnp = [kb.sbuf(f"xnp{ci}", (128, T), BF16) for ci in range(NT)]
    xh = kb.sbuf("xh", (128, NT), F32)
    for ci in range(NT):
        kb.dma("QS", xh, xh.ap[:, ci:ci + 1], w["x_halo"],
               w["x_halo"].ap[ci * 128:(ci + 1) * 128, :])
    sqh = kb.sbuf("sqh", (128, NT), F32)
    kb.act("S", sqh, sqh.ap, xh, xh.ap, AF.Square)
    hstat = ctx.stats_ps
    for ci in range(NT):
        kb.mm(hstat, hstat.ap[:, 0:1], ctx.onesA, ctx.onesA.ap, xh,
              xh.ap[:, ci:ci + 1], ci == 0, False)
        kb.mm(hstat, hstat.ap[:, 0:1], ctx.onesB, ctx.onesB.ap, sqh,
              sqh.ap[:, ci:ci + 1], False, ci == NT - 1)
    hrow = kb.sbuf("hrow", (1, 4), F32)
    kb.ts("V", hrow, hrow.ap[:, 0:1], hstat, hstat.ap[0:1, 0:1], 1.0 / D, ALU.mult)
    kb.ts("V", hrow, hrow.ap[:, 1:2], hstat, hstat.ap[32:33, 0:1], 1.0 / D, ALU.mult)
    kb.tt("V", hrow, hrow.ap[:, 2:3], hrow, hrow.ap[:, 0:1], hrow,
          hrow.ap[:, 0:1], ALU.mult)
    kb.tt("V", hrow, hrow.ap[:, 2:3], hrow, hrow.ap[:, 1:2], hrow,
          hrow.ap[:, 2:3], ALU.subtract)
    eps_c = kb.const(1e-5)
    kb.act("S", hrow, hrow.ap[:, 2:3], hrow, hrow.ap[:, 2:3], AF.Sqrt,
           bias=(eps_c, eps_c.ap[0:1, :]))
    kb.op("V", (lambda v, o=hrow.ap[:, 3:4], i=hrow.ap[:, 2:3]:
                v.reciprocal(o, i)), reads=[hrow], writes=[hrow])
    kb.mm(ctx.bc_ps[0], ctx.bc_ps[0].ap[:, 0:1], ctx.ones_col, ctx.ones_col.ap,
          hrow, hrow.ap[:, 0:1], True, True)
    kb.mm(ctx.bc_ps[1], ctx.bc_ps[1].ap[:, 0:1], ctx.ones_col, ctx.ones_col.ap,
          hrow, hrow.ap[:, 3:4], True, True)
    hsc = kb.sbuf("hsc", (128, NT), F32)
    for ci in range(NT):
        kb.tt("V", hsc, hsc.ap[:, ci:ci + 1], xh, xh.ap[:, ci:ci + 1],
              ctx.bc_ps[0], ctx.bc_ps[0].ap[:, 0:1], ALU.subtract)
        kb.tt("V", xnp[ci], xnp[ci].ap[:, 0:1], hsc, hsc.ap[:, ci:ci + 1],
              ctx.bc_ps[1], ctx.bc_ps[1].ap[:, 0:1], ALU.mult)
    for ci in range(NT):
        kb.copy("V", xnp[ci], xnp[ci].ap[:, 1:T], ctx.xn[ci],
                ctx.xn[ci].ap[:, 0:T - 1], tag="xnp_shift")

    wb = [kb.sbuf(f"l1fw{i}", (128, 1024), BF16) for i in range(2)]
    xr_c = kb.sbuf("xr_c", (128, NT * TCH), BF16)
    xk_c = kb.sbuf("xk_c", (128, NT * TCH), BF16)
    R_c = kb.sbuf("R_c", (128, NT * TCH), BF16)
    ka_c = kb.sbuf("ka_c", (128, NT * TCH), BF16)
    t_c = kb.sbuf("t_c", (128, TCH), F32)

    for tc in range(ctx.ntch):
        sl = slice(tc * TCH, (tc + 1) * TCH)
        for ci in range(NT):
            csl = slice(ci * TCH, (ci + 1) * TCH)
            kb.ts("V", t_c, t_c.ap, ctx.xn[ci], ctx.xn[ci].ap[:, sl],
                  mur[ci], ALU.mult, tag="xr1")
            kb.stt("V", xr_c, xr_c.ap[:, csl], xnp[ci], xnp[ci].ap[:, sl],
                   omr[ci], t_c, t_c.ap, ALU.mult, ALU.add, tag="xr2")
            kb.ts("V", t_c, t_c.ap, ctx.xn[ci], ctx.xn[ci].ap[:, sl],
                  muk[ci], ALU.mult, tag="xk1")
            kb.stt("V", xk_c, xk_c.ap[:, csl], xnp[ci], xnp[ci].ap[:, sl],
                   omk[ci], t_c, t_c.ap, ALU.mult, ALU.add, tag="xk2")
        for oc in range(NT):
            p = oc % 2
            kb.dma("QG", wb[p], wb[p].ap, w["wr"], w["wr"].ap[oc], tag="l1fwr")
            ps = ctx.psum_next()
            for ci in range(NT):
                kb.mm(ps, ps.ap, wb[p], wb[p].ap[:, ci * 128:(ci + 1) * 128],
                      xr_c, xr_c.ap[:, ci * TCH:(ci + 1) * TCH], ci == 0,
                      ci == NT - 1)
            kb.act("S", R_c, R_c.ap[:, oc * TCH:(oc + 1) * TCH], ps, ps.ap,
                   AF.Sigmoid, tag="l1fsig")
        for oc in range(NT):
            p = oc % 2
            kb.dma("QG", wb[p], wb[p].ap, w["wk"], w["wk"].ap[oc], tag="l1fwk")
            ps = ctx.psum_next()
            for ci in range(NT):
                kb.mm(ps, ps.ap, wb[p], wb[p].ap[:, ci * 128:(ci + 1) * 128],
                      xk_c, xk_c.ap[:, ci * TCH:(ci + 1) * TCH], ci == 0,
                      ci == NT - 1)
            osl = slice(oc * TCH, (oc + 1) * TCH)
            kb.act("S", ka_c, ka_c.ap[:, osl], ps, ps.ap, AF.Relu, tag="l1frel")
            kb.tt("V", ka_c, ka_c.ap[:, osl], ka_c, ka_c.ap[:, osl], ka_c,
                  ka_c.ap[:, osl], ALU.mult, tag="l1fsq")
        for oc in range(NT):
            p = oc % 2
            kb.dma("QG", wb[p], wb[p].ap, w["wv"], w["wv"].ap[oc], tag="l1fwv")
            ps = ctx.psum_next()
            for ci in range(NT):
                kb.mm(ps, ps.ap, wb[p], wb[p].ap[:, ci * 128:(ci + 1) * 128],
                      ka_c, ka_c.ap[:, ci * TCH:(ci + 1) * TCH], ci == 0,
                      ci == NT - 1)
            osl = slice(oc * TCH, (oc + 1) * TCH)
            kb.tt("V", t_c, t_c.ap, ps, ps.ap, R_c, R_c.ap[:, osl], ALU.mult,
                  tag="l1fmulR")
            par = oc % 2
            xr_, xo_ = ctx.xr[par], ctx.xo[par]
            kb.dma("QS", xr_, xr_.ap, x_dram,
                   x_dram.ap[oc * 128:(oc + 1) * 128, sl])
            kb.tt("V", xo_, xo_.ap, t_c, t_c.ap, xr_, xr_.ap, ALU.add,
                  tag="l1fres")
            kb.dma("QS", xout_dram, xout_dram.ap[oc * 128:(oc + 1) * 128, sl],
                   xo_, xo_.ap)


def l1_ffn_host_inputs(inputs, x_prev_full=None):
    out = {
        "wr": pack_lhsT(np.ascontiguousarray(np.asarray(inputs["L1_ffn_W_r"]).T)),
        "wk": pack_lhsT(np.ascontiguousarray(np.asarray(inputs["L1_ffn_W_k"]).T)),
        "wv": pack_lhsT(np.ascontiguousarray(np.asarray(inputs["L1_ffn_W_v"]).T)),
        "mu_r": np.asarray(inputs["L1_ffn_mu_r"], np.float32).reshape(D, 1),
        "mu_k": np.asarray(inputs["L1_ffn_mu_k"], np.float32).reshape(D, 1),
    }
    return out


def l1_ffn_halos(x_full):
    xt = x_full.reshape(BATCH * SEQ, D).T
    halos = []
    for i in range(N_CORES):
        t0 = i * T_CORE
        if t0 % SEQ == 0:
            halos.append(np.zeros((D, 1), np.float32))
        else:
            halos.append(np.ascontiguousarray(xt[:, t0 - 1:t0]).astype(np.float32))
    return halos


L1_FFN_WSPEC = {"wr": ((8, 128, 1024), F32), "wk": ((8, 128, 1024), F32),
                "wv": ((8, 128, 1024), F32), "mu_r": ((D, 1), F32),
                "mu_k": ((D, 1), F32), "x_halo": ((D, 1), F32)}



def emit_l2_ffn(kb, ctx, w, x_dram, xout_dram):
    T = ctx.T
    ntt = T // 128
    E = 8
    rw = kb.sbuf("l2rw", (128, 64), BF16)
    kb.dma("QG", rw, rw.ap, w["router_w"], w["router_w"].ap)
    rb = kb.sbuf("l2rb", (1, E), BF16)
    kb.dma("QG", rb, rb.ap, w["router_b"], w["router_b"].ap)
    eye8 = kb.sbuf("eye8", (8, 8 * 128), BF16)
    kb.dma("QG", eye8, eye8.ap, w["eyeblk8"], w["eyeblk8"].ap)
    eye128 = kb.sbuf("eye128", (128, 128), F32)
    kb.dma("QS", eye128, eye128.ap, w["eye128"], w["eye128"].ap)

    mask_rows = kb.sbuf("mask_rows", (8, T), BF16)
    lg = kb.sbuf("lg", (128, E), F32)
    mc = kb.sbuf("mc", (128, E), F32)
    rmax = kb.sbuf("rmax", (128, 1), F32)
    tr_ps = ctx.tr_ps

    for ti in range(ntt):
        tsl = slice(ti * 128, (ti + 1) * 128)
        ps = ctx.psum_next()
        for ci in range(NT):
            kb.mm(ps, ps.ap[:, 0:E], ctx.xn[ci], ctx.xn[ci].ap[:, tsl],
                  rw, rw.ap[:, ci * E:(ci + 1) * E], ci == 0, False, tag="l2rt")
        kb.mm(ps, ps.ap[:, 0:E], ctx.ones_col_bf, ctx.ones_col_bf.ap, rb, rb.ap,
              False, True, tag="l2rb")
        kb.copy("V", lg, lg.ap, ps, ps.ap[:, 0:E], tag="l2lg")
        kb.reduce("V", rmax, rmax.ap, lg, lg.ap, ALU.max, tag="l2max")
        kb.ts("V", mc, mc.ap, lg, lg.ap, rmax, ALU.is_ge, tag="l2mc")
        kb.op("T", (lambda t, o=tr_ps.ap[0:8, 0:128], i=mc.ap, e=eye128.ap:
                    t.transpose(o, i, e)), reads=[mc, eye128], writes=[tr_ps],
              tag="l2tr")
        kb.copy("V", mask_rows, mask_rows.ap[:, tsl], tr_ps,
                tr_ps.ap[0:8, 0:128], tag="l2mrow")

    wa = [kb.sbuf(f"l2wa{i}", (128, 1024), BF16) for i in range(2)]
    wg = [kb.sbuf(f"l2wg{i}", (128, 1024), BF16) for i in range(2)]
    w2 = [kb.sbuf(f"l2w2{i}", (128, 4096), BF16) for i in range(2)]
    b1c = kb.sbuf("l2b1c", (128, 64), F32)
    b2r = kb.sbuf("l2b2r", (8, 1024), BF16)
    kb.dma("QG", b2r, b2r.ap, w["b2"], w["b2"].ap)
    glu = kb.sbuf("l2glu", (128, 32 * 1024), BF16)
    acc = [kb.sbuf(f"l2acc{i}", (128, 1024), BF16) for i in range(NT)]
    sg = kb.sbuf("l2sg", (128, TCH), BF16)
    am = kb.sbuf("l2am", (128, TCH), BF16)
    mb = kb.sbuf("l2mb", (128, 1024), BF16)

    HALF = 1024
    for half in range(T // HALF):
        for e in range(E):
            kb.dma("QS", b1c, b1c.ap, w["b1c"], w["b1c"].ap[e], tag="l2b1")
            for sub in range(2):
                tc0 = half * HALF + sub * TCH
                ps = ctx.psum_next()
                kb.mm(ps, ps.ap, eye8, eye8.ap[:, e * 128:(e + 1) * 128],
                      mask_rows, mask_rows.ap[:, tc0:tc0 + TCH],
                      True, True, tag="l2mbm")
                kb.copy("V", mb, mb.ap[:, sub * TCH:(sub + 1) * TCH], ps, ps.ap,
                        tag="l2mbc")
            for foc in range(32):
                p = foc % 2
                kb.dma("QG", wa[p], wa[p].ap, w["w1p"],
                       w["w1p"].ap[e, foc], tag="l2wa")
                kb.dma("QG", wg[p], wg[p].ap, w["w1p"],
                       w["w1p"].ap[e, 32 + foc], tag="l2wg")
                for sub in range(HALF // TCH):
                    tc0 = half * HALF + sub * TCH
                    sl = slice(tc0, tc0 + TCH)
                    msl = slice(sub * TCH, (sub + 1) * TCH)
                    psa = ctx.psum_next()
                    for ci in range(NT):
                        kb.mm(psa, psa.ap, wa[p],
                              wa[p].ap[:, ci * 128:(ci + 1) * 128],
                              ctx.xn[ci], ctx.xn[ci].ap[:, sl], ci == 0,
                              ci == NT - 1)
                    psg = ctx.psum_next()
                    for ci in range(NT):
                        kb.mm(psg, psg.ap, wg[p],
                              wg[p].ap[:, ci * 128:(ci + 1) * 128],
                              ctx.xn[ci], ctx.xn[ci].ap[:, sl], ci == 0,
                              ci == NT - 1)
                    kb.act("S", sg, sg.ap, psg, psg.ap, AF.Sigmoid,
                           bias=(b1c, b1c.ap[:, 32 + foc:33 + foc]),
                           tag="l2sig")
                    kb.stt("V", am, am.ap, psa, psa.ap,
                           (b1c, b1c.ap[:, foc:foc + 1]),
                           mb, mb.ap[:, msl], ALU.add, ALU.mult, tag="l2am")
                    kb.tt("V", glu, glu.ap[:, foc * 1024 + sub * TCH:
                                           foc * 1024 + (sub + 1) * TCH],
                          am, am.ap, sg, sg.ap, ALU.mult, tag="l2glu")
            for ocd in range(NT):
                p2 = ocd % 2
                kb.dma("QG", w2[p2], w2[p2].ap, w["w2p"], w["w2p"].ap[e, ocd],
                       tag="l2w2")
                for sub in range(HALF // TCH):
                    msl = slice(sub * TCH, (sub + 1) * TCH)
                    ps = ctx.psum_next()
                    for fi in range(32):
                        kb.mm(ps, ps.ap, w2[p2],
                              w2[p2].ap[:, fi * 128:(fi + 1) * 128],
                              glu, glu.ap[:, fi * 1024 + sub * TCH:
                                          fi * 1024 + (sub + 1) * TCH],
                              fi == 0, fi == 31)
                    if e == 0:
                        kb.copy("V", acc[ocd], acc[ocd].ap[:, msl], ps, ps.ap,
                                tag="l2acc0")
                    else:
                        kb.tt("V", acc[ocd], acc[ocd].ap[:, msl], ps, ps.ap,
                              acc[ocd], acc[ocd].ap[:, msl], ALU.add,
                              tag="l2accA")
        for ocd in range(NT):
            for sub in range(HALF // TCH):
                tc0 = half * HALF + sub * TCH
                sl = slice(tc0, tc0 + TCH)
                msl = slice(sub * TCH, (sub + 1) * TCH)
                ps = ctx.psum_next()
                kb.mm(ps, ps.ap, b2r, b2r.ap[:, ocd * 128:(ocd + 1) * 128],
                      mask_rows, mask_rows.ap[:, sl], True, True, tag="l2bm")
                par = (ocd + sub) % 2
                xr_, xo_ = ctx.xr[par], ctx.xo[par]
                kb.dma("QS", xr_, xr_.ap, x_dram,
                       x_dram.ap[ocd * 128:(ocd + 1) * 128, sl])
                kb.tt("V", xo_, xo_.ap, ps, ps.ap, acc[ocd],
                      acc[ocd].ap[:, msl], ALU.add, tag="l2fin1")
                kb.tt("V", xo_, xo_.ap, xo_, xo_.ap, xr_, xr_.ap, ALU.add,
                      tag="l2fin2")
                kb.dma("QS", xout_dram,
                       xout_dram.ap[ocd * 128:(ocd + 1) * 128, sl], xo_, xo_.ap)


def l2_ffn_host_inputs(inputs):
    rw = np.asarray(inputs["L2_ffn_router_w"], np.float32)
    w1 = np.asarray(inputs["L2_ffn_w1"], np.float32)
    w2 = np.asarray(inputs["L2_ffn_w2"], np.float32)
    return {
        "router_w": np.ascontiguousarray(
            rw.reshape(8, 128, 8).transpose(1, 0, 2).reshape(128, 64)),
        "router_b": np.asarray(inputs["L2_ffn_router_b"], np.float32
                               ).reshape(1, 8),
        "w1p": np.stack([pack_lhsT(w1[e]) for e in range(8)]),
        "w2p": np.stack([pack_lhsT(w2[e]) for e in range(8)]),
        "b1c": np.ascontiguousarray(np.stack(
            [np.concatenate([b[:4096].reshape(32, 128).T,
                             b[4096:].reshape(32, 128).T], axis=1)
             for b in np.asarray(inputs["L2_ffn_b1"], np.float32)])),
        "b2": np.asarray(inputs["L2_ffn_b2"], np.float32),
        "eyeblk8": np.kron(np.eye(8, dtype=np.float32), np.ones((1, 128), np.float32)),
        "eye128": np.eye(128, dtype=np.float32),
    }


L2_FFN_WSPEC = {"router_w": ((128, 64), F32), "router_b": ((1, 8), F32),
                "w1p": ((8, 64, 128, 1024), F32), "w2p": ((8, 8, 128, 4096), F32),
                "b1c": ((8, 128, 64), F32), "b2": ((8, 1024), F32),
                "eyeblk8": ((8, 1024), F32), "eye128": ((128, 128), F32)}


def apv(buf, offset, pattern):
    return bass.AP(buf.ap.tensor, offset, pattern)



def emit_l3_ffn(kb, ctx, w, x_dram, xout_dram):
    T = ctx.T
    ntt = T // 128
    E = 16
    rw = kb.sbuf("l3rw", (128, 128), BF16)
    kb.dma("QG", rw, rw.ap, w["router_w"], w["router_w"].ap)
    brow = kb.sbuf("l3brow", (1, E), F32)
    kb.dma("QS", brow, brow.ap, w["bias"], w["bias"].ap)
    eye16 = kb.sbuf("eye16", (16, 16 * 128), BF16)
    kb.dma("QG", eye16, eye16.ap, w["eyeblk16"], w["eyeblk16"].ap)
    eye128 = kb.sbuf("l3eye128", (128, 128), F32)
    kb.dma("QS", eye128, eye128.ap, w["eye128"], w["eye128"].ap)
    neg1 = kb.sbuf("l3neg1", (128, E), F32)
    kb.memset("G", neg1, neg1.ap, -1.0)

    comb_rows = kb.sbuf("comb_rows", (16, T), BF16)
    scr = kb.sbuf("l3scr", (128, E), F32)
    ss = kb.sbuf("l3ss", (128, E), F32)
    msk = kb.sbuf("l3msk", (128, E), F32)
    ex1 = kb.sbuf("l3ex1", (128, E), F32)
    ex2 = kb.sbuf("l3ex2", (128, E), F32)
    ptmp = kb.sbuf("l3ptmp", (128, 32), F32)
    gsb = kb.sbuf("l3gs", (128, 4), F32)
    colA = kb.sbuf("l3colA", (128, 1), F32)
    colB = kb.sbuf("l3colB", (128, 1), F32)
    tw = kb.sbuf("l3tw", (128, E), F32)
    PAIRS = [(0, 1), (0, 2), (0, 3), (1, 2), (1, 3), (2, 3)]

    for ti in range(ntt):
        tsl = slice(ti * 128, (ti + 1) * 128)
        ps = ctx.psum_next()
        for ci in range(NT):
            kb.mm(ps, ps.ap[:, 0:E], ctx.xn[ci], ctx.xn[ci].ap[:, tsl],
                  rw, rw.ap[:, ci * E:(ci + 1) * E], ci == 0, ci == NT - 1,
                  tag="l3rt")
        kb.act("S", scr, scr.ap, ps, ps.ap[:, 0:E], AF.Sigmoid, tag="l3scr")
        kb.mm(ctx.bc_ps[0], ctx.bc_ps[0].ap[:, 0:E], ctx.ones_col,
              ctx.ones_col.ap, brow, brow.ap, True, True, tag="l3bb")
        kb.tt("V", ss, ss.ap, scr, scr.ap, ctx.bc_ps[0],
              ctx.bc_ps[0].ap[:, 0:E], ALU.add, tag="l3s")
        for pi, (i, j) in enumerate(PAIRS):
            kb.op("V", (lambda v, o=apv(ptmp, pi, [[32, 128], [8, 4]]),
                        x=apv(ss, i, [[16, 128], [4, 4]]),
                        y=apv(ss, j, [[16, 128], [4, 4]]):
                        v.tensor_tensor(o, x, y, ALU.add)),
                  reads=[ss], writes=[ptmp], tag="l3pair")
        kb.op("V", (lambda v, o=gsb.ap,
                    i=apv(ptmp, 0, [[32, 128], [8, 4], [1, 6]]):
                    v.tensor_reduce(o, i, AX.X, ALU.max)),
              reads=[ptmp], writes=[gsb], tag="l3gs")
        kb.reduce("V", colA, colA.ap, gsb, gsb.ap, ALU.max, tag="l3g1")
        kb.ts("V", ex1, ex1.ap[:, 0:4], gsb, gsb.ap, colA, ALU.is_ge, tag="l3g2")
        kb.stt("V", ex2, ex2.ap[:, 0:4], ex1, ex1.ap[:, 0:4], -1e9, gsb,
               gsb.ap, ALU.mult, ALU.add, tag="l3g3")
        kb.reduce("V", colB, colB.ap, ex2, ex2.ap[:, 0:4], ALU.max, tag="l3g4")
        kb.ts("V", ex1, ex1.ap[:, 0:4], gsb, gsb.ap, colB, ALU.is_ge, tag="l3gm")
        kb.op("V", (lambda v, o=ex2.ap,
                    i=apv(ex1, 0, [[16, 128], [1, 4], [0, 4]]):
                    v.tensor_copy(o, i)), reads=[ex1], writes=[ex2], tag="l3em")
        kb.ts("V", msk, msk.ap, ss, ss.ap, 1.0, ALU.add, tag="l3sel0")
        kb.tt("V", msk, msk.ap, msk, msk.ap, ex2, ex2.ap, ALU.mult, tag="l3sel1")
        kb.ts("V", msk, msk.ap, msk, msk.ap, -1.0, ALU.add, tag="l3sel2")
        cur = msk
        for it in range(3):
            nxt = ex1 if cur is not ex1 else ex2
            kb.reduce("V", colA, colA.ap, cur, cur.ap, ALU.max, tag=f"l3t{it}a")
            kb.ts("V", tw, tw.ap, cur, cur.ap, colA, ALU.is_ge, tag=f"l3t{it}b")
            kb.stt("V", nxt, nxt.ap, tw, tw.ap, -1e9, cur, cur.ap, ALU.mult,
                   ALU.add, tag=f"l3t{it}c")
            cur = nxt
        kb.reduce("V", colA, colA.ap, cur, cur.ap, ALU.max, tag="l3thr")
        kb.ts("V", ex1, ex1.ap, msk, msk.ap, colA, ALU.is_ge, tag="l3memb")
        kb.tt("V", tw, tw.ap, scr, scr.ap, ex1, ex1.ap, ALU.mult, tag="l3tw")
        kb.reduce("V", colB, colB.ap, tw, tw.ap, ALU.add, tag="l3den")
        kb.ts("V", colB, colB.ap, colB, colB.ap, 1e-20, ALU.add, tag="l3dene")
        kb.op("V", (lambda v, o=colA.ap, i=colB.ap: v.reciprocal(o, i)),
              reads=[colB], writes=[colA], tag="l3inv")
        kb.ts("V", tw, tw.ap, tw, tw.ap, colA, ALU.mult, 2.5, ALU.mult,
              tag="l3wn")
        kb.op("T", (lambda t, o=ctx.tr_ps.ap[0:16, 0:128], i=tw.ap,
                    e=eye128.ap: t.transpose(o, i, e)),
              reads=[tw, eye128], writes=[ctx.tr_ps], tag="l3trp")
        kb.copy("V", comb_rows, comb_rows.ap[:, tsl], ctx.tr_ps,
                ctx.tr_ps.ap[0:16, 0:128], tag="l3crow")

    wg = [kb.sbuf(f"l3wg{i}", (128, 1024), BF16) for i in range(2)]
    wu = [kb.sbuf(f"l3wu{i}", (128, 1024), BF16) for i in range(2)]
    wd = [kb.sbuf(f"l3wd{i}", (128, 1024), BF16) for i in range(2)]
    wd2 = [kb.sbuf(f"l3wd2{i}", (128, 2048), BF16) for i in range(2)]
    glu = kb.sbuf("l3glu", (128, 16 * 1024), BF16)
    acc = [kb.sbuf(f"l3acc{i}", (128, 1024), F32) for i in range(NT)]
    sg = kb.sbuf("l3sg", (128, TCH), BF16)
    um = kb.sbuf("l3um", (128, TCH), BF16)
    cb = kb.sbuf("l3cb", (128, 1024), BF16)

    HALF = 1024
    for half in range(T // HALF):
        first = True
        for e in range(E):
            for sub in range(2):
                tc0 = half * HALF + sub * TCH
                ps = ctx.psum_next()
                kb.mm(ps, ps.ap, eye16, eye16.ap[:, e * 128:(e + 1) * 128],
                      comb_rows, comb_rows.ap[:, tc0:tc0 + TCH], True, True,
                      tag="l3cb")
                kb.copy("V", cb, cb.ap[:, sub * TCH:(sub + 1) * TCH], ps,
                        ps.ap, tag="l3cbc")
            for foc in range(8):
                p = foc % 2
                kb.dma("QG", wg[p], wg[p].ap, w["gatep"], w["gatep"].ap[e, foc],
                       tag="l3wg")
                kb.dma("QG", wu[p], wu[p].ap, w["upp"], w["upp"].ap[e, foc],
                       tag="l3wu")
                for sub in range(2):
                    tc0 = half * HALF + sub * TCH
                    sl = slice(tc0, tc0 + TCH)
                    msl = slice(sub * TCH, (sub + 1) * TCH)
                    psg = ctx.psum_next()
                    for ci in range(NT):
                        kb.mm(psg, psg.ap, wg[p],
                              wg[p].ap[:, ci * 128:(ci + 1) * 128],
                              ctx.xn[ci], ctx.xn[ci].ap[:, sl], ci == 0,
                              ci == NT - 1)
                    psu = ctx.psum_next()
                    for ci in range(NT):
                        kb.mm(psu, psu.ap, wu[p],
                              wu[p].ap[:, ci * 128:(ci + 1) * 128],
                              ctx.xn[ci], ctx.xn[ci].ap[:, sl], ci == 0,
                              ci == NT - 1)
                    kb.act("S", sg, sg.ap, psg, psg.ap, AF.Silu, tag="l3silu")
                    kb.tt("V", um, um.ap, psu, psu.ap, cb, cb.ap[:, msl],
                          ALU.mult, tag="l3um")
                    kb.tt("V", glu, glu.ap[:, foc * 1024 + sub * TCH:
                                           foc * 1024 + (sub + 1) * TCH],
                          um, um.ap, sg, sg.ap, ALU.mult, tag="l3glu")
            for ocd in range(NT):
                p2 = ocd % 2
                kb.dma("QG", wd[p2], wd[p2].ap, w["downp"],
                       w["downp"].ap[e, ocd], tag="l3wd")
                for sub in range(2):
                    msl = slice(sub * TCH, (sub + 1) * TCH)
                    ps = ctx.psum_next()
                    for fi in range(8):
                        kb.mm(ps, ps.ap, wd[p2],
                              wd[p2].ap[:, fi * 128:(fi + 1) * 128],
                              glu, glu.ap[:, fi * 1024 + sub * TCH:
                                          fi * 1024 + (sub + 1) * TCH],
                              fi == 0, fi == 7)
                    if first:
                        kb.copy("V", acc[ocd], acc[ocd].ap[:, msl], ps, ps.ap,
                                tag="l3acc0")
                    else:
                        kb.tt("V", acc[ocd], acc[ocd].ap[:, msl], ps, ps.ap,
                              acc[ocd], acc[ocd].ap[:, msl], ALU.add,
                              tag="l3accA")
            first = False
        for foc in range(16):
            p = foc % 2
            kb.dma("QG", wg[p], wg[p].ap, w["shgp"], w["shgp"].ap[foc],
                   tag="l3shg")
            kb.dma("QG", wu[p], wu[p].ap, w["shup"], w["shup"].ap[foc],
                   tag="l3shu")
            for sub in range(2):
                tc0 = half * HALF + sub * TCH
                sl = slice(tc0, tc0 + TCH)
                psg = ctx.psum_next()
                for ci in range(NT):
                    kb.mm(psg, psg.ap, wg[p],
                          wg[p].ap[:, ci * 128:(ci + 1) * 128],
                          ctx.xn[ci], ctx.xn[ci].ap[:, sl], ci == 0,
                          ci == NT - 1)
                psu = ctx.psum_next()
                for ci in range(NT):
                    kb.mm(psu, psu.ap, wu[p],
                          wu[p].ap[:, ci * 128:(ci + 1) * 128],
                          ctx.xn[ci], ctx.xn[ci].ap[:, sl], ci == 0,
                          ci == NT - 1)
                kb.act("S", sg, sg.ap, psg, psg.ap, AF.Silu, tag="l3ssilu")
                kb.copy("V", um, um.ap, psu, psu.ap, tag="l3sum")
                kb.tt("V", glu, glu.ap[:, foc * 1024 + sub * TCH:
                                       foc * 1024 + (sub + 1) * TCH],
                      um, um.ap, sg, sg.ap, ALU.mult, tag="l3sglu")
        for ocd in range(NT):
            p2 = ocd % 2
            kb.dma("QG", wd2[p2], wd2[p2].ap, w["shdp"], w["shdp"].ap[ocd],
                   tag="l3shd")
            for sub in range(2):
                tc0 = half * HALF + sub * TCH
                sl = slice(tc0, tc0 + TCH)
                msl = slice(sub * TCH, (sub + 1) * TCH)
                ps = ctx.psum_next()
                for fi in range(16):
                    kb.mm(ps, ps.ap, wd2[p2],
                          wd2[p2].ap[:, fi * 128:(fi + 1) * 128],
                          glu, glu.ap[:, fi * 1024 + sub * TCH:
                                      fi * 1024 + (sub + 1) * TCH],
                          fi == 0, fi == 15)
                par = (ocd + sub) % 2
                xr_, xo_ = ctx.xr[par], ctx.xo[par]
                kb.dma("QS", xr_, xr_.ap, x_dram,
                       x_dram.ap[ocd * 128:(ocd + 1) * 128, sl])
                kb.tt("V", xo_, xo_.ap, ps, ps.ap, acc[ocd],
                      acc[ocd].ap[:, msl], ALU.add, tag="l3fin1")
                kb.tt("V", xo_, xo_.ap, xo_, xo_.ap, xr_, xr_.ap, ALU.add,
                      tag="l3fin2")
                kb.dma("QS", xout_dram,
                       xout_dram.ap[ocd * 128:(ocd + 1) * 128, sl], xo_, xo_.ap)


def l3_ffn_host_inputs(inputs):
    rw = np.asarray(inputs["L3_ffn_router_w"], np.float32)
    gate = np.asarray(inputs["L3_ffn_gate_w"], np.float32)
    up = np.asarray(inputs["L3_ffn_up_w"], np.float32)
    down = np.asarray(inputs["L3_ffn_down_w"], np.float32)
    return {
        "router_w": np.ascontiguousarray(
            rw.T.reshape(8, 128, 16).transpose(1, 0, 2).reshape(128, 128)),
        "bias": np.asarray(inputs["L3_ffn_bias"], np.float32).reshape(1, 16),
        "gatep": np.stack([pack_lhsT(gate[e]) for e in range(16)]),
        "upp": np.stack([pack_lhsT(up[e]) for e in range(16)]),
        "downp": np.stack([pack_lhsT(down[e]) for e in range(16)]),
        "shgp": pack_lhsT(np.asarray(inputs["L3_ffn_sh_gate_w"], np.float32)),
        "shup": pack_lhsT(np.asarray(inputs["L3_ffn_sh_up_w"], np.float32)),
        "shdp": pack_lhsT(np.asarray(inputs["L3_ffn_sh_down_w"], np.float32)),
        "eyeblk16": np.kron(np.eye(16, dtype=np.float32),
                            np.ones((1, 128), np.float32)),
        "eye128": np.eye(128, dtype=np.float32),
    }


L3_FFN_WSPEC = {"router_w": ((128, 128), F32), "bias": ((1, 16), F32),
                "gatep": ((16, 8, 128, 1024), F32),
                "upp": ((16, 8, 128, 1024), F32),
                "downp": ((16, 8, 128, 1024), F32),
                "shgp": ((16, 128, 1024), F32), "shup": ((16, 128, 1024), F32),
                "shdp": ((8, 128, 2048), F32),
                "eyeblk16": ((16, 2048), F32), "eye128": ((128, 128), F32)}



RET_C = 128


def _l2_gammas():
    lo = np.log(1.0 / 512.0)
    hi = np.log(1.0 / 32.0)
    return (1.0 - np.exp(np.linspace(lo, hi, 8))).astype(np.float64)


def emit_l2_attn_a(kb, ctx, w, x_dram, ret_out, s_out):
    T = ctx.T
    nch = T // RET_C
    wq = kb.sbuf("wq", (128, 8 * 128), BF16)
    kb.dma("QG", wq, wq.ap, w["wq"], w["wq"].ap)
    wk = kb.sbuf("wk", (128, 8 * 128), BF16)
    kb.dma("QG", wk, wk.ap, w["wk"], w["wk"].ap)
    wv = kb.sbuf("wv", (128, 8 * 128), BF16)
    kb.dma("QG", wv, wv.ap, w["wv"], w["wv"].ap)
    dmask = kb.sbuf("dmask", (128, 8 * 128), BF16)
    kb.dma("QG", dmask, dmask.ap, w["dmaskT"], w["dmaskT"].ap)
    grow = kb.sbuf("grow", (128, 8 * 128), BF16)
    kb.dma("QG", grow, grow.ap, w["grow"], w["grow"].ap)
    gcol = kb.sbuf("gcol", (128, 8), F32)
    kb.dma("QS", gcol, gcol.ap, w["gcol"], w["gcol"].ap)
    gI = kb.sbuf("gI", (128, 8 * 128), BF16)
    kb.dma("QG", gI, gI.ap, w["gI"], w["gI"].ap)

    qT = kb.sbuf("qT", (128, T), BF16)
    kT = kb.sbuf("kT", (128, T), BF16)
    vtm = kb.sbuf("vtm", (128, T), BF16)
    kgtm = kb.sbuf("kgtm", (128, T), BF16)
    qg = kb.sbuf("qg", (128, RET_C), BF16)
    At = kb.sbuf("At", (128, RET_C), BF16)
    S_sb = kb.sbuf("S_sb", (128, 128), BF16)
    S_f32 = kb.sbuf("S_f32", (128, 128), F32)
    retf = kb.sbuf("retf", (128, RET_C), F32)

    for h in range(8):
        hsl = slice(h * 128, (h + 1) * 128)
        for tc in range(ctx.ntch):
            sl = slice(tc * TCH, (tc + 1) * TCH)
            ps = ctx.psum_next()
            kb.mm(ps, ps.ap, wq, wq.ap[:, hsl], ctx.xn[h],
                  ctx.xn[h].ap[:, sl], True, True, tag="l2aq")
            kb.copy("S", qT, qT.ap[:, sl], ps, ps.ap, tag="l2aqe")
            ps = ctx.psum_next()
            kb.mm(ps, ps.ap, wk, wk.ap[:, hsl], ctx.xn[h],
                  ctx.xn[h].ap[:, sl], True, True, tag="l2ak")
            kb.copy("S", kT, kT.ap[:, sl], ps, ps.ap, tag="l2ake")
        for tt_ in range(T // 128):
            tsl = slice(tt_ * 128, (tt_ + 1) * 128)
            ps = ctx.psum_next()
            kb.mm(ps, ps.ap[:, 0:128], ctx.xn[h], ctx.xn[h].ap[:, tsl],
                  wv, wv.ap[:, hsl], True, True, tag="l2av")
            kb.copy("S", vtm, vtm.ap[:, tsl], ps, ps.ap[:, 0:128], tag="l2ave")
            ps = ctx.psum_next()
            kb.mm(ps, ps.ap[:, 0:128], ctx.xn[h], ctx.xn[h].ap[:, tsl],
                  wk, wk.ap[:, hsl], True, True, tag="l2aktm")
            kb.ts("V", kgtm, kgtm.ap[:, tsl], ps, ps.ap[:, 0:128],
                  (gcol, gcol.ap[:, h:h + 1]), ALU.mult, tag="l2akg")
        kb.memset("V", S_sb, S_sb.ap, 0.0)
        for c in range(nch):
            csl = slice(c * RET_C, (c + 1) * RET_C)
            ps = ctx.psum_next()
            kb.mm(ps, ps.ap[:, 0:RET_C], kT, kT.ap[:, csl], qT, qT.ap[:, csl],
                  True, True, tag="l2asc")
            kb.tt("V", At, At.ap, ps, ps.ap[:, 0:RET_C], dmask,
                  dmask.ap[:, hsl], ALU.mult, tag="l2amask")
            kb.tt("V", qg, qg.ap, qT, qT.ap[:, csl], grow, grow.ap[:, hsl],
                  ALU.mult, tag="l2aqg")
            ps = ctx.psum_next()
            kb.mm(ps, ps.ap[:, 0:RET_C], vtm, vtm.ap[:, csl], At, At.ap,
                  True, False, tag="l2ard")
            kb.mm(ps, ps.ap[:, 0:RET_C], S_sb, S_sb.ap, qg, qg.ap,
                  False, True, tag="l2arf")
            kb.copy("V", retf, retf.ap, ps, ps.ap[:, 0:RET_C], tag="l2arete")
            kb.dma("QS", None, ret_out.ap[hsl, csl], retf, retf.ap,
                   tag="l2aretst")
            ps = ctx.psum_next()
            kb.mm(ps, ps.ap[:, 0:128], kgtm, kgtm.ap[:, csl], vtm,
                  vtm.ap[:, csl], True, False, tag="l2asu")
            kb.mm(ps, ps.ap[:, 0:128], gI, gI.ap[:, hsl], S_sb, S_sb.ap,
                  False, True, tag="l2asd")
            if c < nch - 1:
                kb.copy("V", S_sb, S_sb.ap, ps, ps.ap[:, 0:128], tag="l2ase")
            else:
                kb.copy("V", S_f32, S_f32.ap, ps, ps.ap[:, 0:128], tag="l2asf")
                kb.dma("QS", None, s_out.ap[hsl, :], S_f32, S_f32.ap,
                       tag="l2asst")


def emit_l2_attn_b(kb, ctx, w, x_dram, ret_in, spre_in, xout_dram):
    T = ctx.T
    nch = T // RET_C
    gammas = _l2_gammas()
    wq = kb.sbuf("wqB", (128, 8 * 128), BF16)
    kb.dma("QG", wq, wq.ap, w["wq"], w["wq"].ap)
    grow = kb.sbuf("growB", (128, 8 * 128), BF16)
    kb.dma("QG", grow, grow.ap, w["grow"], w["grow"].ap)
    sd = kb.sbuf("sdall", (128, 64), F32)
    kb.dma("QS", sd, sd.ap, w["sdecay"], w["sdecay"].ap)
    gnw = kb.sbuf("gnw", (128, 8), F32)
    kb.dma("QS", gnw, gnw.ap, w["gnw"], w["gnw"].ap)
    gnb = kb.sbuf("gnb", (128, 8), F32)
    kb.dma("QS", gnb, gnb.ap, w["gnb"], w["gnb"].ap)

    ret = [kb.sbuf(f"retB{h}", (128, T), BF16) for h in range(8)]
    for h in range(8):
        kb.dma("QG", ret[h], ret[h].ap, ret_in,
               ret_in.ap[h * 128:(h + 1) * 128, :], tag="l2bret")

    sj = kb.sbuf("sj", (128, 128), F32)
    sstate = kb.sbuf("sstate", (128, 128), BF16)
    qps = kb.sbuf("qpsB", (128, RET_C), BF16)
    qg = kb.sbuf("qgB", (128, RET_C), BF16)

    eps_c = kb.const(1e-5)
    for h in range(8):
        hsl = slice(h * 128, (h + 1) * 128)
        first = True
        for j in range(N_CORES):
            kb.dma("QS", sj, sj.ap, spre_in, spre_in.ap[j, h * 128:(h + 1) * 128, :],
                   tag="l2bsj")
            if first:
                kb.ts("V", ctx.sc[0], ctx.sc[0].ap[:, 0:128], sj, sj.ap,
                      (sd, sd.ap[:, j * 8 + h:j * 8 + h + 1]), ALU.mult, tag="l2bs0")
                first = False
            else:
                kb.stt("V", ctx.sc[0], ctx.sc[0].ap[:, 0:128], sj, sj.ap,
                       (sd, sd.ap[:, j * 8 + h:j * 8 + h + 1]), ctx.sc[0],
                       ctx.sc[0].ap[:, 0:128], ALU.mult, ALU.add, tag="l2bsA")
        kb.copy("V", sstate, sstate.ap, ctx.sc[0], ctx.sc[0].ap[:, 0:128],
                tag="l2bsc")
        for c in range(nch):
            csl = slice(c * RET_C, (c + 1) * RET_C)
            ps = ctx.psum_next()
            kb.mm(ps, ps.ap[:, 0:RET_C], wq, wq.ap[:, hsl], ctx.xn[h],
                  ctx.xn[h].ap[:, csl], True, True, tag="l2bq")
            kb.copy("S", qps, qps.ap, ps, ps.ap[:, 0:RET_C], tag="l2bqe")
            kb.tt("V", qg, qg.ap, qps, qps.ap, grow, grow.ap[:, hsl],
                  ALU.mult, tag="l2bqg")
            ps = ctx.psum_next()
            kb.mm(ps, ps.ap[:, 0:RET_C], sstate, sstate.ap, qg, qg.ap,
                  True, True, tag="l2bmiss")
            kb.tt("V", ret[h], ret[h].ap[:, csl], ps, ps.ap[:, 0:RET_C],
                  ret[h], ret[h].ap[:, csl], ALU.add, tag="l2badd")
            if c < nch - 1:
                kb.ts("V", sstate, sstate.ap, sstate, sstate.ap,
                      float(gammas[h] ** RET_C), ALU.mult, tag="l2bsdec")
        for tc in range(ctx.ntch):
            sl = slice(tc * TCH, (tc + 1) * TCH)
            sq = ctx.sc[0]
            kb.act("S", sq, sq.ap, ret[h], ret[h].ap[:, sl], AF.Square,
                   tag="gn_sq")
            kb.mm(ctx.stats_ps, ctx.stats_ps.ap, ctx.onesA_bf,
                  ctx.onesA_bf.ap, ret[h], ret[h].ap[:, sl], True, False,
                  tag="gn_sum")
            kb.mm(ctx.stats_ps, ctx.stats_ps.ap, ctx.onesB, ctx.onesB.ap,
                  sq, sq.ap, False, True, tag="gn_sumsq")
            RA, RB = ctx.rowsA, ctx.rowsB
            m_ap = RA.ap[0:1, sl]
            ms_ap = RA.ap[64:65, sl]
            vD_ap = RA.ap[32:33, sl]
            r_ap = RB.ap[0:1, sl]
            kb.ts("V", RA, m_ap, ctx.stats_ps, ctx.stats_ps.ap[0:1, :],
                  1.0 / 128, ALU.mult, tag="gn_m")
            kb.act("S", RA, ms_ap, RA, m_ap, AF.Square, tag="gn_msq")
            kb.ts("V", RA, ms_ap, RA, ms_ap, -128.0, ALU.mult, tag="gn_msqD")
            kb.tt("V", RA, vD_ap, ctx.stats_ps, ctx.stats_ps.ap[32:33, :],
                  RA, ms_ap, ALU.add, tag="gn_var")
            kb.act("S", RA, vD_ap, RA, vD_ap, AF.Sqrt, bias=(eps_c, eps_c.ap[0:1, :]),
                   scale=1.0 / 128, tag="gn_sqrt")
            kb.op("V", (lambda vec, o=r_ap, i=vD_ap: vec.reciprocal(o, i)),
                  reads=[RA], writes=[RB], tag="gn_recip")
            kb.mm(ctx.bc_ps[0], ctx.bc_ps[0].ap, ctx.ones_col, ctx.ones_col.ap,
                  RA, m_ap, True, True, tag="gn_bcm")
            kb.mm(ctx.bc_ps[1], ctx.bc_ps[1].ap, ctx.ones_col, ctx.ones_col.ap,
                  RB, r_ap, True, True, tag="gn_bcr")
            t1 = ctx.sc[0]
            kb.tt("V", t1, t1.ap, ret[h], ret[h].ap[:, sl], ctx.bc_ps[0],
                  ctx.bc_ps[0].ap, ALU.subtract, tag="gn_sub")
            kb.tt("V", t1, t1.ap, t1, t1.ap, ctx.bc_ps[1], ctx.bc_ps[1].ap,
                  ALU.mult, tag="gn_mul")
            kb.ts("V", ret[h], ret[h].ap[:, sl], t1, t1.ap,
                  (gnw, gnw.ap[:, h:h + 1]), ALU.mult,
                  (gnb, gnb.ap[:, h:h + 1]), ALU.add, tag="gn_apply")

    wb = [kb.sbuf(f"l2bw{i}", (128, 1024), BF16) for i in range(2)]
    gated = [kb.sbuf(f"gated{i}", (128, T), BF16) for i in range(NT)]
    for tc in range(ctx.ntch):
        sl = slice(tc * TCH, (tc + 1) * TCH)
        for oc in range(NT):
            p = oc % 2
            kb.dma("QG", wb[p], wb[p].ap, w["w1p"], w["w1p"].ap[oc], tag="l2bw1")
            ps = ctx.psum_next()
            for ci in range(NT):
                kb.mm(ps, ps.ap, wb[p], wb[p].ap[:, ci * 128:(ci + 1) * 128],
                      ctx.xn[ci], ctx.xn[ci].ap[:, sl], ci == 0, ci == NT - 1)
            kb.act("S", gated[oc], gated[oc].ap[:, sl], ps, ps.ap, AF.Silu,
                   tag="l2bsilu")
            kb.tt("V", gated[oc], gated[oc].ap[:, sl], gated[oc],
                  gated[oc].ap[:, sl], ret[oc], ret[oc].ap[:, sl], ALU.add,
                  tag="l2bou")
    for tc in range(ctx.ntch):
        sl = slice(tc * TCH, (tc + 1) * TCH)
        for oc in range(NT):
            p = oc % 2
            kb.dma("QG", wb[p], wb[p].ap, w["w2p"], w["w2p"].ap[oc], tag="l2bw2")
            ps = ctx.psum_next()
            for ci in range(NT):
                kb.mm(ps, ps.ap, wb[p], wb[p].ap[:, ci * 128:(ci + 1) * 128],
                      gated[ci], gated[ci].ap[:, sl], ci == 0, ci == NT - 1)
            par = oc % 2
            xr_, xo_ = ctx.xr[par], ctx.xo[par]
            kb.dma("QS", xr_, xr_.ap, x_dram,
                   x_dram.ap[oc * 128:(oc + 1) * 128, sl])
            kb.tt("V", xo_, xo_.ap, ps, ps.ap, xr_, xr_.ap, ALU.add,
                  tag="l2bres")
            kb.dma("QS", xout_dram, xout_dram.ap[oc * 128:(oc + 1) * 128, sl],
                   xo_, xo_.ap)


def l2_attn_host_inputs(inputs):
    g = _l2_gammas()
    C = RET_C
    wq = np.asarray(inputs["L2_attn_Wq"], np.float32)
    wq_l = np.ascontiguousarray(wq.transpose(1, 0, 2).reshape(128, 8 * 128))
    wk_l = np.ascontiguousarray(np.asarray(inputs["L2_attn_Wk"], np.float32)
                                .transpose(1, 0, 2).reshape(128, 8 * 128))
    wv_l = np.ascontiguousarray(np.asarray(inputs["L2_attn_Wv"], np.float32)
                                .transpose(1, 0, 2).reshape(128, 8 * 128))
    n = np.arange(C)
    dm = np.zeros((8, C, C), np.float64)
    for h in range(8):
        diff = n[None, :] - n[:, None]
        dm[h] = np.where(diff >= 0, g[h] ** np.maximum(diff, 0), 0.0)
    dmT = np.ascontiguousarray(dm.transpose(1, 0, 2).reshape(C, 8 * C))
    growm = np.zeros((8, C), np.float64)
    for h in range(8):
        growm[h] = g[h] ** (n + 1)
    growb = np.broadcast_to(growm.reshape(1, 8 * C), (128, 8 * C))
    gcol = np.stack([g[h] ** (C - 1 - np.arange(128)) for h in range(8)],
                    axis=1)
    gI = np.zeros((8, 128, 128), np.float64)
    for h in range(8):
        gI[h] = np.eye(128) * (g[h] ** C)
    gIb = np.ascontiguousarray(gI.transpose(1, 0, 2).reshape(128, 8 * 128))
    sdecays = []
    for i in range(N_CORES):
        li = i % (SEQ // T_CORE)
        sdec = np.zeros((8, 8), np.float64)
        base = (i // (SEQ // T_CORE)) * (SEQ // T_CORE)
        for j in range(N_CORES):
            if base <= j < base + (SEQ // T_CORE) and j < i:
                d = (li - 1) - (j - base)
                sdec[j] = g ** (T_CORE * d)
        sdall = np.zeros((128, 64), np.float64)
        for j in range(N_CORES):
            sdall[:, j * 8:(j + 1) * 8] = sdec[j][None, :]
        sdecays.append(sdall.astype(np.float32))
    return {
        "wq": wq_l, "wk": wk_l, "wv": wv_l,
        "dmaskT": dmT.astype(np.float32),
        "grow": np.ascontiguousarray(growb).astype(np.float32),
        "gcol": gcol.astype(np.float32),
        "gI": gIb.astype(np.float32),
        "gnw": np.ascontiguousarray(
            np.asarray(inputs["L2_attn_gn_w"], np.float32).reshape(8, 128).T),
        "gnb": np.ascontiguousarray(
            np.asarray(inputs["L2_attn_gn_b"], np.float32).reshape(8, 128).T),
        "w1p": pack_lhsT(np.asarray(inputs["L2_attn_W1"], np.float32)),
        "w2p": pack_lhsT(np.asarray(inputs["L2_attn_W2"], np.float32)),
    }, sdecays


L2A_WSPEC = {"wq": ((128, 1024), F32), "wk": ((128, 1024), F32),
             "wv": ((128, 1024), F32), "dmaskT": ((128, 1024), F32),
             "grow": ((128, 1024), F32), "gcol": ((128, 8), F32),
             "gI": ((128, 1024), F32)}
L2B_WSPEC = {"wq": ((128, 1024), F32), "grow": ((128, 1024), F32),
             "sdecay": ((128, 64), F32), "gnw": ((128, 8), F32),
             "gnb": ((128, 8), F32), "w1p": ((8, 128, 1024), F32),
             "w2p": ((8, 128, 1024), F32)}


def build_l2a_launch():
    kb = KB()
    x_in = kb.dram_in("x_in", (D, T_CORE), F32)
    ret_out = kb.dram_out("ret_out", (D, T_CORE), F32)
    s_out = kb.dram_out("s_out", (1024, 128), F32)
    w = {nm: kb.dram_in(nm, s, dt) for nm, (s, dt) in L2A_WSPEC.items()}
    ctx = Ctx(kb)
    emit_ln(kb, ctx, x_in)
    emit_l2_attn_a(kb, ctx, w, x_in, ret_out, s_out)
    return kb.build()


def build_l2b_launch():
    kb = KB()
    x_in = kb.dram_in("x_in", (D, T_CORE), F32)
    ret_in = kb.dram_in("ret_in", (D, T_CORE), F32)
    spre_in = kb.dram_in("spre", (N_CORES, 1024, 128), F32)
    x_out = kb.dram_out("x_out", (D, T_CORE), F32)
    w = {nm: kb.dram_in(nm, s, dt) for nm, (s, dt) in L2B_WSPEC.items()}
    ctx = Ctx(kb)
    emit_ln(kb, ctx, x_in)
    emit_l2_attn_b(kb, ctx, w, x_in, ret_in, spre_in, x_out)
    return kb.build()



def _emit_l0_rot_base(kb, ctx, w):
    mt = kb.sbuf("l0mt", (128, 256), F32)
    a_sb = kb.sbuf("l0a", (128, 2), F32)
    kb.dma("QS", a_sb, a_sb.ap, w["a2"], w["a2"].ap)
    om = kb.sbuf("l0om", (128, 4), F32)
    for kt in range(4):
        kb.dma("QS", mt, mt.ap, w["mtp"], w["mtp"].ap[kt], tag="l0mt")
        ps = ctx.psum_next()
        for j in range(2):
            kb.mm(ps, ps.ap[:, 0:1], mt, mt.ap[:, j * 128:(j + 1) * 128],
                  a_sb, a_sb.ap[:, j:j + 1], j == 0, j == 1, tag="l0om")
        kb.copy("V", om, om.ap[:, kt:kt + 1], ps, ps.ap[:, 0:1], tag="l0ome")
    osq = kb.sbuf("l0osq", (128, 4), F32)
    kb.act("S", osq, osq.ap, om, om.ap, AF.Square, tag="l0osq")
    den = kb.sbuf("l0den", (128, 4), F32)
    kb.ts("V", den, den.ap, osq, osq.ap, 1.0, ALU.add, tag="l0den")
    iden = kb.sbuf("l0iden", (128, 4), F32)
    kb.op("V", (lambda v, o=iden.ap, i=den.ap: v.reciprocal(o, i)),
          reads=[den], writes=[iden], tag="l0iden")
    c1 = kb.sbuf("l0c1", (128, 4), F32)
    kb.ts("V", c1, c1.ap, osq, osq.ap, -1.0, ALU.mult, 1.0, ALU.add, tag="l0c1")
    kb.tt("V", c1, c1.ap, c1, c1.ap, iden, iden.ap, ALU.mult, tag="l0c1b")
    s1 = kb.sbuf("l0s1", (128, 4), F32)
    kb.ts("V", s1, s1.ap, om, om.ap, -2.0, ALU.mult, tag="l0s1")
    kb.tt("V", s1, s1.ap, s1, s1.ap, iden, iden.ap, ALU.mult, tag="l0s1b")

    NP = 34
    rp = kb.sbuf("l0rpow", (128, 4 * NP), F32)
    sc = kb.sbuf("l0rsc", (128, 4), F32)
    osel = kb.sbuf("l0osel", (128, 4), F32)
    kb.dma("QS", osel, osel.ap, w["offsel"], w["offsel"].ap)

    def C(nt, j):
        return rp.ap[:, nt * NP + 2 * j:nt * NP + 2 * j + 1]

    def S(nt, j):
        return rp.ap[:, nt * NP + 2 * j + 1:nt * NP + 2 * j + 2]

    for nt in range(4):
        kb.copy("V", rp, C(nt, 0), c1, c1.ap[:, nt:nt + 1], tag="rp0")
        kb.copy("V", rp, S(nt, 0), s1, s1.ap[:, nt:nt + 1], tag="rp0")
        for j in range(11):
            kb.tt("V", sc, sc.ap[:, 0:1], rp, C(nt, j), rp, S(nt, j),
                  ALU.mult, tag="rpa")
            kb.tt("V", sc, sc.ap[:, 1:2], rp, S(nt, j), rp, S(nt, j),
                  ALU.mult, tag="rpb")
            kb.tt("V", rp, C(nt, j + 1), rp, C(nt, j), rp, C(nt, j),
                  ALU.mult, tag="rpc")
            kb.tt("V", rp, C(nt, j + 1), rp, C(nt, j + 1), sc,
                  sc.ap[:, 1:2], ALU.subtract, tag="rpd")
            kb.ts("V", rp, S(nt, j + 1), sc, sc.ap[:, 0:1], 2.0, ALU.mult,
                  tag="rpe")
        kb.memset("V", rp, rp.ap[:, nt * NP + 24:nt * NP + 25], 1.0)
        kb.memset("V", rp, rp.ap[:, nt * NP + 25:nt * NP + 26], 0.0)
        for kpow in range(3):
            b = nt * NP + 24 + 2 * kpow
            bn = b + 2
            kb.tt("V", sc, sc.ap[:, 0:1], rp, rp.ap[:, b:b + 1], rp,
                  C(nt, 11), ALU.mult, tag="ofa")
            kb.tt("V", sc, sc.ap[:, 1:2], rp, rp.ap[:, b + 1:b + 2], rp,
                  S(nt, 11), ALU.mult, tag="ofb")
            kb.tt("V", rp, rp.ap[:, bn:bn + 1], sc, sc.ap[:, 0:1], sc,
                  sc.ap[:, 1:2], ALU.subtract, tag="ofc")
            kb.tt("V", sc, sc.ap[:, 0:1], rp, rp.ap[:, b:b + 1], rp,
                  S(nt, 11), ALU.mult, tag="ofd")
            kb.tt("V", sc, sc.ap[:, 1:2], rp, rp.ap[:, b + 1:b + 2], rp,
                  C(nt, 11), ALU.mult, tag="ofe")
            kb.tt("V", rp, rp.ap[:, bn + 1:bn + 2], sc, sc.ap[:, 0:1], sc,
                  sc.ap[:, 1:2], ALU.add, tag="off")
        base = nt * NP
        kb.op("V", (lambda v, o=sc.ap[:, 0:4],
                    x=apv(rp, base + 24, [[4 * NP, 128], [2, 4]]),
                    y=osel.ap: v.tensor_tensor(o, x, y, ALU.mult)),
              reads=[rp, osel], writes=[sc], tag="ofsel1")
        kb.reduce("V", rp, rp.ap[:, base + 32:base + 33], sc, sc.ap[:, 0:4],
                  ALU.add, tag="ofsel2")
        kb.op("V", (lambda v, o=sc.ap[:, 0:4],
                    x=apv(rp, base + 25, [[4 * NP, 128], [2, 4]]),
                    y=osel.ap: v.tensor_tensor(o, x, y, ALU.mult)),
              reads=[rp, osel], writes=[sc], tag="ofsel3")
        kb.reduce("V", rp, rp.ap[:, base + 33:base + 34], sc, sc.ap[:, 0:4],
                  ALU.add, tag="ofsel4")
    return rp, NP


def _emit_l0_tile_tables(kb, ctx, rp, NP, nt, ctb, stb):
    T = ctx.T
    base = nt * NP
    kb.copy("V", ctb, ctb.ap[:, 0:1], rp, rp.ap[:, base + 32:base + 33],
            tag="tb0")
    kb.copy("V", stb, stb.ap[:, 0:1], rp, rp.ap[:, base + 33:base + 34],
            tag="tb0")
    sc = ctx.sc[0]
    negS = ctx.xr[0]
    size = 1
    j = 0
    while size < T:
        nxt = min(size, T - size)
        Ck = (rp, rp.ap[:, base + 2 * j:base + 2 * j + 1])
        Sk = (rp, rp.ap[:, base + 2 * j + 1:base + 2 * j + 2])
        kb.ts("V", negS, negS.ap[:, 0:1], rp,
              rp.ap[:, base + 2 * j + 1:base + 2 * j + 2], -1.0, ALU.mult,
              tag="tbn")
        kb.ts("V", sc, sc.ap[:, 0:nxt] if nxt <= TCH else sc.ap,
              ctb, ctb.ap[:, 0:nxt], Ck, ALU.mult, tag="tb1") \
            if nxt <= TCH else None
        if nxt <= TCH:
            kb.stt("V", ctb, ctb.ap[:, size:size + nxt], stb,
                   stb.ap[:, 0:nxt], (negS, negS.ap[:, 0:1]), sc,
                   sc.ap[:, 0:nxt], ALU.mult, ALU.add, tag="tb2")
            kb.ts("V", sc, sc.ap[:, 0:nxt], stb, stb.ap[:, 0:nxt], Ck,
                  ALU.mult, tag="tb3")
            kb.stt("V", stb, stb.ap[:, size:size + nxt], ctb,
                   ctb.ap[:, 0:nxt], Sk, sc, sc.ap[:, 0:nxt], ALU.mult,
                   ALU.add, tag="tb4")
        else:
            for q0 in range(0, nxt, TCH):
                qn = min(TCH, nxt - q0)
                kb.ts("V", sc, sc.ap[:, 0:qn], ctb, ctb.ap[:, q0:q0 + qn],
                      Ck, ALU.mult, tag="tb1b")
                kb.stt("V", ctb, ctb.ap[:, size + q0:size + q0 + qn], stb,
                       stb.ap[:, q0:q0 + qn], (negS, negS.ap[:, 0:1]), sc,
                       sc.ap[:, 0:qn], ALU.mult, ALU.add, tag="tb2b")
                kb.ts("V", sc, sc.ap[:, 0:qn], stb, stb.ap[:, q0:q0 + qn],
                      Ck, ALU.mult, tag="tb3b")
                kb.stt("V", stb, stb.ap[:, size + q0:size + q0 + qn], ctb,
                       ctb.ap[:, q0:q0 + qn], Sk, sc, sc.ap[:, 0:qn],
                       ALU.mult, ALU.add, tag="tb4b")
        size *= 2
        j += 1


def emit_l0_attn_a(kb, ctx, w, x_dram, y_out, tot_out):
    T = ctx.T
    rp, NP = _emit_l0_rot_base(kb, ctx, w)
    bwp = [kb.sbuf(f"l0bw{i}", (128, 1024), BF16) for i in range(2)]
    cwp = [kb.sbuf(f"l0cw{i}", (128, 512), BF16) for i in range(2)]
    dcol = kb.sbuf("l0d", (128, 8), F32)
    kb.dma("QS", dcol, dcol.ap, w["dcol"], w["dcol"].ap)
    ctb = kb.sbuf("l0ctb", (128, T), F32)
    stb = kb.sbuf("l0stb", (128, T), F32)
    bu = kb.sbuf("l0bu", (128, T), F32)
    cbu = kb.sbuf("l0cbu", (128, T), F32)
    sbu = kb.sbuf("l0sbu", (128, T), F32)
    pc = kb.sbuf("l0pc", (128, T), F32)
    psn = kb.sbuf("l0psn", (128, T), F32)
    hbf = [kb.sbuf(f"l0h{i}", (128, T), BF16) for i in range(4)]
    zc = kb.sbuf("l0zc", (128, 1), F32)
    kb.memset("V", zc, zc.ap, 0.0)
    tots = kb.sbuf("l0tot", (128, 2), F32)

    for nt in range(4):
        kb.dma("QG", bwp[nt % 2], bwp[nt % 2].ap, w["bwp"], w["bwp"].ap[nt],
               tag="l0bw")
        for tc in range(ctx.ntch):
            sl = slice(tc * TCH, (tc + 1) * TCH)
            ps = ctx.psum_next()
            for ci in range(NT):
                kb.mm(ps, ps.ap, bwp[nt % 2],
                      bwp[nt % 2].ap[:, ci * 128:(ci + 1) * 128],
                      ctx.xn[ci], ctx.xn[ci].ap[:, sl], ci == 0, ci == NT - 1,
                      tag="l0bu")
            kb.copy("S", bu, bu.ap[:, sl], ps, ps.ap, tag="l0bue")
        _emit_l0_tile_tables(kb, ctx, rp, NP, nt, ctb, stb)
        kb.tt("V", cbu, cbu.ap, ctb, ctb.ap, bu, bu.ap, ALU.mult, tag="l0cbu")
        kb.tt("V", sbu, sbu.ap, stb, stb.ap, bu, bu.ap, ALU.mult, tag="l0sbu")
        kb.op("V", (lambda v, o=pc.ap, a=cbu.ap, z=zc.ap:
                    v.tensor_tensor_scan(o, a, a, z, ALU.add, ALU.bypass)),
              reads=[cbu, zc], writes=[pc], tag="l0scanC")
        kb.op("V", (lambda v, o=psn.ap, a=sbu.ap, z=zc.ap:
                    v.tensor_tensor_scan(o, a, a, z, ALU.add, ALU.bypass)),
              reads=[sbu, zc], writes=[psn], tag="l0scanS")
        kb.copy("V", tots, tots.ap[:, 0:1], pc, pc.ap[:, T - 1:T], tag="l0t0")
        kb.copy("V", tots, tots.ap[:, 1:2], psn, psn.ap[:, T - 1:T], tag="l0t1")
        kb.dma("QS", None, tot_out.ap[nt * 128:(nt + 1) * 128, :], tots,
               tots.ap, tag="l0tst")
        kb.tt("V", cbu, cbu.ap, ctb, ctb.ap, pc, pc.ap, ALU.mult, tag="l0h1")
        kb.tt("V", sbu, sbu.ap, stb, stb.ap, psn, psn.ap, ALU.mult, tag="l0h2")
        kb.tt("V", hbf[nt], hbf[nt].ap, cbu, cbu.ap, sbu, sbu.ap, ALU.add,
              tag="l0h3")
    for oc in range(NT):
        kb.dma("QG", cwp[oc % 2], cwp[oc % 2].ap, w["cwp"], w["cwp"].ap[oc],
               tag="l0cw")
        for tc in range(ctx.ntch):
            sl = slice(tc * TCH, (tc + 1) * TCH)
            ps = ctx.psum_next()
            for ni in range(4):
                kb.mm(ps, ps.ap, cwp[oc % 2],
                      cwp[oc % 2].ap[:, ni * 128:(ni + 1) * 128],
                      hbf[ni], hbf[ni].ap[:, sl], ni == 0, ni == 3, tag="l0y")
            xo_ = ctx.xo[tc % 2]
            kb.stt("V", xo_, xo_.ap, ctx.xn[oc], ctx.xn[oc].ap[:, sl],
                   (dcol, dcol.ap[:, oc:oc + 1]), ps, ps.ap, ALU.mult,
                   ALU.add, tag="l0du")
            kb.dma("QS", None, y_out.ap[oc * 128:(oc + 1) * 128, sl], xo_,
                   xo_.ap, tag="l0yst")


def emit_l0_attn_b(kb, ctx, w, x_dram, y_in, tots_in, xout_dram):
    T = ctx.T
    rp, NP = _emit_l0_rot_base(kb, ctx, w)
    ctb = kb.sbuf("l0ctbB", (128, T), F32)
    stb = kb.sbuf("l0stbB", (128, T), F32)
    tj = kb.sbuf("l0tj", (128, 2), F32)
    carry = kb.sbuf("l0carry", (128, 2), F32)
    pmask = kb.sbuf("l0pm", (128, 8), F32)
    kb.dma("QS", pmask, pmask.ap, w["pmask"], w["pmask"].ap)
    hbf = [kb.sbuf(f"l0hB{i}", (128, T), BF16) for i in range(4)]
    cwp = [kb.sbuf(f"l0cwB{i}", (128, 512), BF16) for i in range(2)]
    yst = kb.sbuf("l0yst", (128, TCH), F32)

    for nt in range(4):
        _emit_l0_tile_tables(kb, ctx, rp, NP, nt, ctb, stb)
        kb.memset("V", carry, carry.ap, 0.0)
        for j in range(N_CORES):
            kb.dma("QS", tj, tj.ap, tots_in,
                   tots_in.ap[j, nt * 128:(nt + 1) * 128, :], tag="l0tj")
            kb.stt("V", carry, carry.ap, tj, tj.ap,
                   (pmask, pmask.ap[:, j:j + 1]), carry, carry.ap, ALU.mult,
                   ALU.add, tag="l0cacc")
        t1 = ctx.sc[0]
        for q0 in range(0, T, TCH):
            kb.ts("V", t1, t1.ap, ctb, ctb.ap[:, q0:q0 + TCH],
                  (carry, carry.ap[:, 0:1]), ALU.mult, tag="l0dh1")
            kb.stt("V", hbf[nt], hbf[nt].ap[:, q0:q0 + TCH], stb,
                   stb.ap[:, q0:q0 + TCH], (carry, carry.ap[:, 1:2]), t1,
                   t1.ap, ALU.mult, ALU.add, tag="l0dh2")
    for oc in range(NT):
        kb.dma("QG", cwp[oc % 2], cwp[oc % 2].ap, w["cwp"], w["cwp"].ap[oc],
               tag="l0cwB")
        for tc in range(ctx.ntch):
            sl = slice(tc * TCH, (tc + 1) * TCH)
            ps = ctx.psum_next()
            for ni in range(4):
                kb.mm(ps, ps.ap, cwp[oc % 2],
                      cwp[oc % 2].ap[:, ni * 128:(ni + 1) * 128],
                      hbf[ni], hbf[ni].ap[:, sl], ni == 0, ni == 3, tag="l0yB")
            par = tc % 2
            xr_, xo_ = ctx.xr[par], ctx.xo[par]
            kb.dma("QS", yst, yst.ap, y_in, y_in.ap[oc * 128:(oc + 1) * 128, sl],
                   tag="l0ldy")
            kb.dma("QS", xr_, xr_.ap, x_dram,
                   x_dram.ap[oc * 128:(oc + 1) * 128, sl], tag="l0ldx")
            kb.tt("V", xo_, xo_.ap, ps, ps.ap, yst, yst.ap, ALU.add,
                  tag="l0resA")
            kb.tt("V", xo_, xo_.ap, xo_, xo_.ap, xr_, xr_.ap, ALU.add,
                  tag="l0resB")
            kb.dma("QS", xout_dram, xout_dram.ap[oc * 128:(oc + 1) * 128, sl],
                   xo_, xo_.ap, tag="l0stx")


def l0_attn_host_inputs(inputs):
    n = 512
    k = np.arange(n)
    m = np.arange(256)
    M = -(np.sin(2 * np.pi * np.outer(k, (m + 1)) / n)
          + np.where(m[None, :] >= 1,
                     np.sin(2 * np.pi * np.outer(k, m) / n), 0.0))
    B_w = np.asarray(inputs["L0_attn_B_w"], np.float32)
    C_w = np.asarray(inputs["L0_attn_C_w"], np.float32)
    a = np.asarray(inputs["L0_attn_a_params"], np.float32)
    base = {
        "mtp": pack_lhsT(np.ascontiguousarray(M.T).astype(np.float32)),
        "a2": np.ascontiguousarray(a.reshape(2, 128).T),
        "bwp": pack_lhsT(np.ascontiguousarray(B_w.T)),
        "cwp": pack_lhsT(np.ascontiguousarray(C_w.T)),
        "dcol": np.ascontiguousarray(
            np.asarray(inputs["L0_attn_D"], np.float32).reshape(8, 128).T),
    }
    offsels, pmasks = [], []
    ncb = SEQ // T_CORE
    for i in range(N_CORES):
        li = i % ncb
        osel = np.zeros((128, 4), np.float32)
        osel[:, li] = 1.0
        pm = np.zeros((128, 8), np.float32)
        base_c = (i // ncb) * ncb
        for j in range(base_c, i):
            pm[:, j] = 1.0
        offsels.append(osel)
        pmasks.append(pm)
    return base, offsels, pmasks


L0A_WSPEC = {"mtp": ((4, 128, 256), F32), "a2": ((128, 2), F32),
             "bwp": ((4, 128, 1024), F32), "cwp": ((8, 128, 512), F32),
             "dcol": ((128, 8), F32), "offsel": ((128, 4), F32)}
L0B_WSPEC = {"mtp": ((4, 128, 256), F32), "a2": ((128, 2), F32),
             "cwp": ((8, 128, 512), F32), "offsel": ((128, 4), F32),
             "pmask": ((128, 8), F32)}


def build_l0a_launch():
    kb = KB()
    x_in = kb.dram_in("x_in", (D, T_CORE), F32)
    y_out = kb.dram_out("y_out", (D, T_CORE), F32)
    tot_out = kb.dram_out("tot_out", (512, 2), F32)
    w = {nm: kb.dram_in(nm, s, dt) for nm, (s, dt) in L0A_WSPEC.items()}
    ctx = Ctx(kb)
    emit_ln(kb, ctx, x_in)
    emit_l0_attn_a(kb, ctx, w, x_in, y_out, tot_out)
    return kb.build()


def build_l0b_launch():
    kb = KB()
    x_in = kb.dram_in("x_in", (D, T_CORE), F32)
    y_in = kb.dram_in("y_in", (D, T_CORE), F32)
    tots_in = kb.dram_in("tots", (N_CORES, 512, 2), F32)
    x_out = kb.dram_out("x_out", (D, T_CORE), F32)
    w = {nm: kb.dram_in(nm, s, dt) for nm, (s, dt) in L0B_WSPEC.items()}
    ctx = Ctx(kb)
    emit_l0_attn_b(kb, ctx, w, x_in, y_in, tots_in, x_out)
    return kb.build()



L3A_J = 16
L3A_HALO = 16


def emit_ln_small(kb, ctx, src_buf, ncol, out_writer, eps=1e-5):
    eps_c = kb.const(eps)
    sq = ctx.sc[0]
    for ci in range(NT):
        asl = src_buf.ap[:, ci * ncol:(ci + 1) * ncol]
        kb.act("S", sq, sq.ap[:, 0:ncol], src_buf, asl, AF.Square, tag="lns_sq")
        kb.mm(ctx.stats_ps, ctx.stats_ps.ap[:, 0:ncol], ctx.onesA,
              ctx.onesA.ap, src_buf, asl, ci == 0, False, tag="lns_sum")
        kb.mm(ctx.stats_ps, ctx.stats_ps.ap[:, 0:ncol], ctx.onesB,
              ctx.onesB.ap, sq, sq.ap[:, 0:ncol], False, ci == NT - 1,
              tag="lns_sumsq")
    RA, RB = ctx.rowsA, ctx.rowsB
    m_ap = RA.ap[0:1, 0:ncol]
    ms_ap = RA.ap[64:65, 0:ncol]
    vD_ap = RA.ap[32:33, 0:ncol]
    r_ap = RB.ap[0:1, 0:ncol]
    kb.ts("V", RA, m_ap, ctx.stats_ps, ctx.stats_ps.ap[0:1, 0:ncol], 1.0 / D,
          ALU.mult, tag="lns_m")
    kb.act("S", RA, ms_ap, RA, m_ap, AF.Square, tag="lns_msq")
    kb.ts("V", RA, ms_ap, RA, ms_ap, -float(D), ALU.mult, tag="lns_msqD")
    kb.tt("V", RA, vD_ap, ctx.stats_ps, ctx.stats_ps.ap[32:33, 0:ncol], RA,
          ms_ap, ALU.add, tag="lns_var")
    kb.act("S", RA, vD_ap, RA, vD_ap, AF.Sqrt, bias=(eps_c, eps_c.ap[0:1, :]),
           scale=1.0 / D, tag="lns_sqrt")
    kb.op("V", (lambda vec, o=r_ap, i=vD_ap: vec.reciprocal(o, i)),
          reads=[RA], writes=[RB], tag="lns_recip")
    kb.mm(ctx.bc_ps[0], ctx.bc_ps[0].ap[:, 0:ncol], ctx.ones_col,
          ctx.ones_col.ap, RA, m_ap, True, True, tag="lns_bcm")
    kb.mm(ctx.bc_ps[1], ctx.bc_ps[1].ap[:, 0:ncol], ctx.ones_col,
          ctx.ones_col.ap, RB, r_ap, True, True, tag="lns_bcr")
    t1 = ctx.sc[0]
    for ci in range(NT):
        asl = src_buf.ap[:, ci * ncol:(ci + 1) * ncol]
        kb.tt("V", t1, t1.ap[:, 0:ncol], src_buf, asl, ctx.bc_ps[0],
              ctx.bc_ps[0].ap[:, 0:ncol], ALU.subtract, tag="lns_sub")
        kb.tt("V", t1, t1.ap[:, 0:ncol], t1, t1.ap[:, 0:ncol], ctx.bc_ps[1],
              ctx.bc_ps[1].ap[:, 0:ncol], ALU.mult, tag="lns_mul")
        out_writer(ci, t1)


def emit_l3_attn(kb, ctx, w, x_dram, xout_dram):
    T = ctx.T
    J = L3A_J
    H = L3A_HALO
    NI = 16
    TE = T + H
    Aall = kb.sbuf("l3aA", (128, 16 * 256), BF16)
    kb.dma("QG", Aall, Aall.ap, w["Ap"], w["Ap"].ap)
    Ball = kb.sbuf("l3aB", (128, 256), F32)
    kb.dma("QS", Ball, Ball.ap, w["Bp"], w["Bp"].ap)
    Call = kb.sbuf("l3aC", (128, 256), BF16)
    kb.dma("QG", Call, Call.ap, w["Cp"], w["Cp"].ap)
    binc = kb.sbuf("l3bin", (128, 16), F32)
    kb.dma("QS", binc, binc.ap, w["b_in"], w["b_in"].ap)
    bgc = kb.sbuf("l3bg", (128, 16), F32)
    kb.dma("QS", bgc, bgc.ap, w["b_gate"], w["b_gate"].ap)
    boc = kb.sbuf("l3bo", (128, 8), F32)
    kb.dma("QS", boc, boc.ap, w["b_out"], w["b_out"].ap)
    hmask = kb.sbuf("l3hm", (128, 1), F32)
    kb.dma("QS", hmask, hmask.ap, w["hmask"], w["hmask"].ap)

    hst = kb.sbuf("l3hst", (128, NT * H), F32)
    for ci in range(NT):
        kb.dma("QS", hst, hst.ap[:, ci * H:(ci + 1) * H], w["x_halo"],
               w["x_halo"].ap[ci * 128:(ci + 1) * 128, :], tag="l3hld")
    xnh = kb.sbuf("l3xnh", (128, NT * H), BF16)

    def hwr(ci, t1):
        kb.copy("V", xnh, xnh.ap[:, ci * H:(ci + 1) * H], t1,
                t1.ap[:, 0:H], tag="l3xnh")

    emit_ln_small(kb, ctx, hst, H, hwr)

    xp = [kb.sbuf(f"l3xp{i}", (128, TE), BF16) for i in range(NI)]
    wfi = [kb.sbuf(f"l3wf{i}", (128, 1024), BF16) for i in range(2)]
    for fi in range(NI):
        p = fi % 2
        kb.dma("QG", wfi[p], wfi[p].ap, w["w_inp"], w["w_inp"].ap[fi],
               tag="l3wi")
        ps = ctx.psum_next()
        for ci in range(NT):
            kb.mm(ps, ps.ap[:, 0:H], wfi[p],
                  wfi[p].ap[:, ci * 128:(ci + 1) * 128],
                  xnh, xnh.ap[:, ci * H:(ci + 1) * H], ci == 0, ci == NT - 1,
                  tag="l3xph")
        t1 = ctx.sc[0]
        kb.ts("V", t1, t1.ap[:, 0:H], ps, ps.ap[:, 0:H],
              (binc, binc.ap[:, fi:fi + 1]), ALU.add, tag="l3xphb")
        kb.ts("V", xp[fi], xp[fi].ap[:, 0:H], t1, t1.ap[:, 0:H],
              (hmask, hmask.ap), ALU.mult, tag="l3xphm")
        for tc in range(ctx.ntch):
            sl = slice(tc * TCH, (tc + 1) * TCH)
            ps = ctx.psum_next()
            for ci in range(NT):
                kb.mm(ps, ps.ap, wfi[p], wfi[p].ap[:, ci * 128:(ci + 1) * 128],
                      ctx.xn[ci], ctx.xn[ci].ap[:, sl], ci == 0, ci == NT - 1,
                      tag="l3xpm")
            kb.ts("V", xp[fi], xp[fi].ap[:, H + tc * TCH:H + (tc + 1) * TCH],
                  ps, ps.ap, (binc, binc.ap[:, fi:fi + 1]), ALU.add,
                  tag="l3xpe")

    v2 = [kb.sbuf(f"l3v{i}", (128, 256), F32) for i in range(2)]
    prod = kb.sbuf("l3prod", (128, 256), F32)
    kap = kb.sbuf("l3kap", (128, 16 * J), F32)
    kb.copy("V", v2[0], v2[0].ap, Ball, Ball.ap, tag="l3v0")
    for j in range(J):
        v = v2[j % 2]
        kb.tt("V", prod, prod.ap, v, v.ap, Call, Call.ap, ALU.mult,
              tag="l3kp")
        kb.op("V", (lambda vec, o=apv(kap, j, [[16 * J, 128], [J, 16]]),
                    i=apv(prod, 0, [[256, 128], [16, 16], [1, 16]]):
                    vec.tensor_reduce(o, i, AX.X, ALU.add)),
              reads=[prod], writes=[kap], tag="l3kr")
        if j == J - 1:
            break
        vn = v2[(j + 1) % 2]
        for m in range(16):
            kb.op("V", (lambda vec, o=prod.ap,
                        a=apv(Aall, m, [[4096, 128], [256, 16], [16, 16]]),
                        b=apv(v, m, [[256, 128], [16, 16], [0, 16]]):
                        vec.tensor_tensor(o, a, b, ALU.mult)),
                  reads=[Aall, v], writes=[prod], tag="l3vm")
            if m == 0:
                kb.copy("V", vn, vn.ap, prod, prod.ap, tag="l3vc")
            else:
                kb.tt("V", vn, vn.ap, vn, vn.ap, prod, prod.ap, ALU.add,
                      tag="l3va")

    ych = kb.sbuf("l3ych", (128, TCH), F32)
    gch = kb.sbuf("l3gch", (128, TCH), BF16)
    yg = [kb.sbuf(f"l3yg{i}", (128, TCH), BF16) for i in range(NI)]
    wgf = [kb.sbuf(f"l3wgf{i}", (128, 1024), BF16) for i in range(2)]
    wof = [kb.sbuf(f"l3wof{i}", (128, 2048), BF16) for i in range(2)]
    for tc in range(ctx.ntch):
        sl = slice(tc * TCH, (tc + 1) * TCH)
        for fi in range(NI):
            base = H + tc * TCH
            kcol = lambda j: (kap, kap.ap[:, fi * J + j:fi * J + j + 1])
            kb.ts("V", ych, ych.ap, xp[fi], xp[fi].ap[:, base:base + TCH],
                  kcol(0), ALU.mult, tag="l3cv0")
            for j in range(1, J):
                kb.stt("V", ych, ych.ap, xp[fi],
                       xp[fi].ap[:, base - j:base - j + TCH], kcol(j), ych,
                       ych.ap, ALU.mult, ALU.add, tag=f"l3cv{j}")
            p = fi % 2
            kb.dma("QG", wgf[p], wgf[p].ap, w["w_gatep"], w["w_gatep"].ap[fi],
                   tag="l3wg")
            ps = ctx.psum_next()
            for ci in range(NT):
                kb.mm(ps, ps.ap, wgf[p], wgf[p].ap[:, ci * 128:(ci + 1) * 128],
                      ctx.xn[ci], ctx.xn[ci].ap[:, sl], ci == 0, ci == NT - 1,
                      tag="l3gm")
            kb.act("S", gch, gch.ap, ps, ps.ap, AF.Sigmoid,
                   bias=(bgc, bgc.ap[:, fi:fi + 1]), tag="l3ge")
            kb.tt("V", yg[fi], yg[fi].ap, ych, ych.ap, gch, gch.ap, ALU.mult,
                  tag="l3yg")
        for oc in range(NT):
            p = oc % 2
            kb.dma("QG", wof[p], wof[p].ap, w["w_outp"], w["w_outp"].ap[oc],
                   tag="l3wo")
            ps = ctx.psum_next()
            for fi in range(NI):
                kb.mm(ps, ps.ap, wof[p], wof[p].ap[:, fi * 128:(fi + 1) * 128],
                      yg[fi], yg[fi].ap, fi == 0, fi == NI - 1, tag="l3om")
            par = oc % 2
            xr_, xo_ = ctx.xr[par], ctx.xo[par]
            kb.dma("QS", xr_, xr_.ap, x_dram,
                   x_dram.ap[oc * 128:(oc + 1) * 128, sl])
            kb.stt("V", xo_, xo_.ap, ps, ps.ap, (boc, boc.ap[:, oc:oc + 1]),
                   xr_, xr_.ap, ALU.add, ALU.add, tag="l3res")
            kb.dma("QS", xout_dram, xout_dram.ap[oc * 128:(oc + 1) * 128, sl],
                   xo_, xo_.ap)


def l3_attn_host_inputs(inputs):
    A = np.asarray(inputs["L3_attn_A"], np.float32)
    B = np.asarray(inputs["L3_attn_B_ssm"], np.float32)
    C = np.asarray(inputs["L3_attn_C_ssm"], np.float32)
    Ap = np.ascontiguousarray(
        A.reshape(16, 128, 256).transpose(1, 0, 2).reshape(128, 16 * 256))
    Bp = np.ascontiguousarray(
        B.reshape(16, 128, 16).transpose(1, 0, 2).reshape(128, 256))
    Cp = np.ascontiguousarray(
        C.reshape(16, 128, 16).transpose(1, 0, 2).reshape(128, 256))
    base = {
        "Ap": Ap, "Bp": Bp, "Cp": Cp,
        "w_inp": pack_lhsT(np.asarray(inputs["L3_attn_w_in"], np.float32)),
        "w_gatep": pack_lhsT(np.asarray(inputs["L3_attn_w_gate"], np.float32)),
        "w_outp": pack_lhsT(np.asarray(inputs["L3_attn_w_out"], np.float32)),
        "b_in": np.ascontiguousarray(
            np.asarray(inputs["L3_attn_b_in"], np.float32).reshape(16, 128).T),
        "b_gate": np.ascontiguousarray(
            np.asarray(inputs["L3_attn_b_gate"], np.float32).reshape(16, 128).T),
        "b_out": np.ascontiguousarray(
            np.asarray(inputs["L3_attn_b_out"], np.float32).reshape(8, 128).T),
    }
    return base


def l3_attn_halos(x_full):
    xt = x_full.reshape(BATCH * SEQ, D).T
    halos, hmasks = [], []
    for i in range(N_CORES):
        t0 = i * T_CORE
        if t0 % SEQ == 0:
            halos.append(np.zeros((D, L3A_HALO), np.float32))
            hmasks.append(np.zeros((128, 1), np.float32))
        else:
            halos.append(np.ascontiguousarray(
                xt[:, t0 - L3A_HALO:t0]).astype(np.float32))
            hmasks.append(np.ones((128, 1), np.float32))
    return halos, hmasks


L3A_WSPEC = {"Ap": ((128, 4096), F32), "Bp": ((128, 256), F32),
             "Cp": ((128, 256), F32),
             "w_inp": ((16, 128, 1024), F32), "w_gatep": ((16, 128, 1024), F32),
             "w_outp": ((8, 128, 2048), F32), "b_in": ((128, 16), F32),
             "b_gate": ((128, 16), F32), "b_out": ((128, 8), F32),
             "x_halo": ((D, L3A_HALO), F32), "hmask": ((128, 1), F32)}



L1_HALO = 512
L1_CH = 128
L1_H = 32
L1_P = 64
L1_N = 128
L1_INTER = 2048
L1_NI = 16


def emit_l1_attn(kb, ctx, w, x_dram, xout_dram):
    import os
    STAGE = int(os.environ.get("L1_STAGE", "99"))
    T = ctx.T
    TE = T + L1_HALO
    NG = TE // TCH
    NCHG = TCH // L1_CH
    eps_c = kb.const(1e-6)
    one_c = kb.const(1.0)

    hst = ctx.xst[0]
    xnh = [kb.sbuf(f"l1xnh{i}", (128, L1_HALO), BF16) for i in range(NT)]
    for ci in range(NT):
        kb.dma("QS", hst, hst.ap[:, ci * TCH:(ci + 1) * TCH], w["x_halo"],
               w["x_halo"].ap[ci * 128:(ci + 1) * 128, :], tag="l1hld")

    def hwr(ci, t1):
        kb.copy("V", xnh[ci], xnh[ci].ap, t1, t1.ap[:, 0:TCH], tag="l1xnh")

    emit_ln_small(kb, ctx, hst, TCH, hwr)

    def xn_ext(ci, g):
        if g == 0:
            return xnh[ci], xnh[ci].ap
        sl = slice((g - 1) * TCH, g * TCH)
        return ctx.xn[ci], ctx.xn[ci].ap[:, sl]

    convw = kb.sbuf("l1cw", (128, 18 * 4), F32)
    kb.dma("QS", convw, convw.ap, w["convw"], w["convw"].ap)
    convb = kb.sbuf("l1cb", (128, 18), F32)
    kb.dma("QS", convb, convb.ap, w["convb"], w["convb"].ap)
    dtb = kb.sbuf("l1dtb", (32, 1), F32)
    kb.dma("QS", dtb, dtb.ap, w["dt_bias"], w["dt_bias"].ap)
    alog = kb.sbuf("l1alog", (32, 1), F32)
    kb.dma("QS", alog, alog.ap, w["a_log"], w["a_log"].ap)
    negA = kb.sbuf("l1negA", (32, 1), F32)
    kb.act("S", negA, negA.ap, alog, alog.ap, AF.Exp, tag="l1A")
    kb.ts("V", negA, negA.ap, negA, negA.ap, -1.0, ALU.mult, tag="l1An")
    dcol = kb.sbuf("l1D", (128, 16), F32)
    kb.dma("QS", dcol, dcol.ap, w["dpack"], w["dpack"].ap)
    nrmw = kb.sbuf("l1nw", (128, 16), F32)
    kb.dma("QS", nrmw, nrmw.ap, w["nwpack"], w["nwpack"].ap)
    hmask = kb.sbuf("l1hm", (128, 1), F32)
    kb.dma("QS", hmask, hmask.ap, w["hmask"], w["hmask"].ap)
    trilm = kb.sbuf("l1tril", (128, 128), BF16)
    kb.dma("QG", trilm, trilm.ap, w["trilm"], w["trilm"].ap)
    eye128 = kb.sbuf("l1eye", (128, 128), F32)
    kb.dma("QS", eye128, eye128.ap, w["eye128"], w["eye128"].ap)
    eye128b = kb.sbuf("l1eyeb", (128, 128), BF16)
    kb.dma("QG", eye128b, eye128b.ap, w["eye128"], w["eye128"].ap)

    wout = [kb.sbuf(f"l1wo{i}", (128, 2048), BF16) for i in range(NT)]
    for oc in range(NT):
        kb.dma("QG", wout[oc], wout[oc].ap, w["outp"], w["outp"].ap[oc],
               tag="l1wol")

    gate_g = kb.sbuf("l1gate", (128, 16 * TCH), BF16)
    hbca_g = kb.sbuf("l1hbca", (128, 18 * TCH), BF16)
    z_g = kb.sbuf("l1z", (128, TCH), F32)
    tails = kb.sbuf("l1tail", (128, 18 * 4), F32)
    newt = kb.sbuf("l1ntail", (128, 18 * 4), F32)
    kb.memset("V", tails, tails.ap, 0.0)
    dt_g = kb.sbuf("l1dt", (32, TCH), F32)
    dsc = kb.sbuf("l1dsc", (32, TCH), F32)
    dsc2 = kb.sbuf("l1dsc2", (32, TCH), F32)
    acum_g = kb.sbuf("l1acum", (32, TCH), F32)
    acumT_g = kb.sbuf("l1acumT", (128, NCHG * 32), F32)
    zc32 = kb.sbuf("l1z32", (32, 1), F32)
    kb.memset("V", zc32, zc32.ap, 0.0)
    rowA = ctx.xst[0]
    rowAl = kb.sbuf("l1rowAl", (1, 32), F32)
    decT = kb.sbuf("l1decT", (128, 32), F32)
    dtotb = kb.sbuf("l1dtot", (128, 32), F32)
    dtT = kb.sbuf("l1dtT", (128, NCHG * 32), F32)
    diffs = kb.sbuf("l1diff", (128, 32 * L1_CH), BF16)
    expbc = kb.sbuf("l1expbc", (128, 32 * L1_CH), BF16)
    s0t = kb.sbuf("l1s0t", (128, L1_CH), BF16)
    hstm = kb.sbuf("l1hstm", (128, L1_INTER), BF16)
    hstmd = kb.sbuf("l1hstmd", (128, L1_INTER), BF16)
    bmtm = kb.sbuf("l1bmtm", (128, L1_CH), BF16)
    S_sb = kb.sbuf("l1S", (128, L1_H * L1_P), BF16)
    kb.memset("V", S_sb, S_sb.ap, 0.0)
    Y_c = kb.sbuf("l1Y", (128, 16 * L1_CH), BF16)
    yn_c = kb.sbuf("l1yn", (128, 16 * L1_CH), BF16)
    sqs = kb.sbuf("l1sq", (128, L1_CH), F32)
    wbuf = [kb.sbuf(f"l1wb{i}", (128, 1024), BF16) for i in range(2)]
    acum_dram = kb.dram_tmp("l1acum_d", (32, L1_CH), F32)
    wdt = kb.sbuf("l1wdt", (128, 256), BF16)
    kb.dma("QG", wdt, wdt.ap, w["wdtp"], w["wdtp"].ap, tag="l1wdt")

    for g in range(NG):
        for oc in range(34):
            p = oc % 2
            kb.dma("QG", wbuf[p], wbuf[p].ap, w["inp"], w["inp"].ap[oc],
                   tag="l1wi")
            ps = ctx.psum_next()
            for ci in range(NT):
                xb, xap = xn_ext(ci, g)
                kb.mm(ps, ps.ap, wbuf[p], wbuf[p].ap[:, ci * 128:(ci + 1) * 128],
                      xb, xap, ci == 0, ci == NT - 1, tag="l1pm")
            if oc < 16:
                kb.act("S", gate_g, gate_g.ap[:, oc * TCH:(oc + 1) * TCH],
                       ps, ps.ap, AF.Silu, tag="l1gact")
            else:
                ti = oc - 16
                kb.copy("V", z_g, z_g.ap, ps, ps.ap, tag="l1zc")
                csl = slice(ti * TCH, (ti + 1) * TCH)
                kb.ts("V", hbca_g, hbca_g.ap[:, csl], z_g, z_g.ap,
                      (convw, convw.ap[:, ti * 4 + 3:ti * 4 + 4]), ALU.mult,
                      tag="l1cv3")
                for k in range(3):
                    sh = 3 - k
                    kb.op("V", (lambda v, o=hbca_g.ap[:, ti * TCH + sh:
                                                      (ti + 1) * TCH],
                                a=z_g.ap[:, 0:TCH - sh],
                                s=(convw.ap[:, ti * 4 + k:ti * 4 + k + 1]),
                                b=hbca_g.ap[:, ti * TCH + sh:(ti + 1) * TCH]:
                                v.scalar_tensor_tensor(o, a, s, b, ALU.mult,
                                                       ALU.add)),
                          reads=[z_g, convw, hbca_g], writes=[hbca_g],
                          tag="l1cvk")
                    kb.op("V", (lambda v, o=hbca_g.ap[:, ti * TCH:
                                                      ti * TCH + sh],
                                a=tails.ap[:, ti * 4 + k + 1:ti * 4 + k + 1 + sh],
                                s=(convw.ap[:, ti * 4 + k:ti * 4 + k + 1]),
                                b=hbca_g.ap[:, ti * TCH:ti * TCH + sh]:
                                v.scalar_tensor_tensor(o, a, s, b, ALU.mult,
                                                       ALU.add)),
                          reads=[tails, convw, hbca_g], writes=[hbca_g],
                          tag="l1cvt")
                kb.copy("V", newt, newt.ap[:, ti * 4:ti * 4 + 4], z_g,
                        z_g.ap[:, TCH - 4:TCH], tag="l1tl")
                kb.ts("V", hbca_g, hbca_g.ap[:, csl], hbca_g,
                      hbca_g.ap[:, csl], (convb, convb.ap[:, ti:ti + 1]),
                      ALU.add, tag="l1cvb")
                kb.act("S", hbca_g, hbca_g.ap[:, csl], hbca_g,
                       hbca_g.ap[:, csl], AF.Silu, tag="l1cvs")
                if g == 0:
                    kb.ts("V", hbca_g, hbca_g.ap[:, csl], hbca_g,
                          hbca_g.ap[:, csl], (hmask, hmask.ap), ALU.mult,
                          tag="l1hmask")
        kb.copy("V", tails, tails.ap, newt, newt.ap, tag="l1tl2")
        if STAGE < 2:
            continue
        ps = ctx.psum_next()
        for ci in range(NT):
            xb, xap = xn_ext(ci, g)
            kb.mm(ps, ps.ap[0:32, :], wdt, wdt.ap[:, ci * 32:(ci + 1) * 32],
                  xb, xap, ci == 0, ci == NT - 1, tag="l1dtm")
        kb.ts("V", dt_g, dt_g.ap, ps, ps.ap[0:32, :], (dtb, dtb.ap), ALU.add,
              tag="l1dta")
        kb.act("S", dsc, dsc.ap, dt_g, dt_g.ap, AF.Abs, tag="l1sp1")
        kb.act("S", dsc, dsc.ap, dsc, dsc.ap, AF.Exp, scale=-1.0, tag="l1sp2")
        kb.act("S", dsc, dsc.ap, dsc, dsc.ap, AF.Ln,
               bias=(one_c, one_c.ap[0:32, :]), tag="l1sp3")
        kb.op("V", (lambda v, o=dsc2.ap, i=dt_g.ap: v.tensor_relu(o, i)),
              reads=[dt_g], writes=[dsc2], tag="l1sp4")
        kb.tt("V", dt_g, dt_g.ap, dsc, dsc.ap, dsc2, dsc2.ap, ALU.add,
              tag="l1sp5")
        kb.ts("V", dsc, dsc.ap, dt_g, dt_g.ap, (negA, negA.ap), ALU.mult,
              tag="l1adt")
        for cc in range(NCHG):
            lsl = slice(cc * L1_CH, (cc + 1) * L1_CH)
            kb.op("V", (lambda v, o=acum_g.ap[:, lsl], a=dsc.ap[:, lsl],
                        z=zc32.ap: v.tensor_tensor_scan(o, a, a, z, ALU.add,
                                                        ALU.bypass)),
                  reads=[dsc, zc32], writes=[acum_g], tag="l1scan")
            kb.op("T", (lambda t, o=ctx.tr_ps.ap[0:128, 0:32],
                        i=acum_g.ap[:, lsl], e=eye128.ap[0:32, 0:32]:
                        t.transpose(o, i, e)),
                  reads=[acum_g, eye128], writes=[ctx.tr_ps], tag="l1acT")
            kb.copy("V", acumT_g, acumT_g.ap[:, cc * 32:(cc + 1) * 32],
                    ctx.tr_ps, ctx.tr_ps.ap[0:128, 0:32], tag="l1acTe")
            kb.op("T", (lambda t, o=ctx.tr_ps.ap[0:128, 0:32],
                        i=dt_g.ap[:, lsl], e=eye128.ap[0:32, 0:32]:
                        t.transpose(o, i, e)),
                  reads=[dt_g, eye128], writes=[ctx.tr_ps], tag="l1dtT")
            kb.copy("V", dtT, dtT.ap[:, cc * 32:(cc + 1) * 32],
                    ctx.tr_ps, ctx.tr_ps.ap[0:128, 0:32], tag="l1dtTe")
        if STAGE < 3:
            continue
        for cc in range(NCHG):
            c_glob = g * NCHG + cc
            lsl = slice(cc * L1_CH, (cc + 1) * L1_CH)
            asl = slice(cc * 32, (cc + 1) * 32)
            kb.dma("QS", acum_dram, acum_dram.ap, acum_g, acum_g.ap[:, lsl],
                   tag="l1rowA1")
            kb.dma("QS", rowA, rowA.ap[0:1, 0:32 * L1_CH], acum_dram,
                   acum_dram.ap.rearrange("h l -> (h l)").rearrange("(o x) -> o x", o=1), tag="l1rowA2")
            kb.op("T", (lambda t, o=ctx.tr_ps.ap[0:1, 0:32],
                        i=acum_g.ap[:, cc * L1_CH + L1_CH - 1:
                                    cc * L1_CH + L1_CH],
                        e=eye128.ap[0:32, 0:32]:
                        t.transpose(o, i, e)),
                  reads=[acum_g, eye128], writes=[ctx.tr_ps], tag="l1AlT")
            kb.copy("V", rowAl, rowAl.ap, ctx.tr_ps, ctx.tr_ps.ap[0:1, 0:32],
                    tag="l1Ale")
            kb.mm(ctx.bc_ps[0], ctx.bc_ps[0].ap[:, 0:32], ctx.ones_col,
                  ctx.ones_col.ap, rowAl, rowAl.ap, True, True, tag="l1Alb")
            kb.tt("V", decT, decT.ap, ctx.bc_ps[0], ctx.bc_ps[0].ap[:, 0:32],
                  acumT_g, acumT_g.ap[:, asl], ALU.subtract, tag="l1dec1")
            kb.act("S", decT, decT.ap, decT, decT.ap, AF.Exp, tag="l1dec2")
            kb.act("S", dtotb, dtotb.ap, ctx.bc_ps[0],
                   ctx.bc_ps[0].ap[:, 0:32], AF.Exp, tag="l1dtot")
            if STAGE < 35:
                continue
            for q in range(8):
                qsl = slice(q * TCH, (q + 1) * TCH)
                psb = ctx.bc_ps[q % 2]
                kb.mm(psb, psb.ap, ctx.ones_col, ctx.ones_col.ap, rowA,
                      rowA.ap[0:1, qsl], True, True, tag="l1bcl")
                if STAGE == 361:
                    continue
                hq = q * 4
                for hh in range(4):
                    h = hq + hh
                    kb.ts("V", diffs,
                          diffs.ap[:, h * L1_CH:(h + 1) * L1_CH],
                          psb, psb.ap[:, hh * L1_CH:(hh + 1) * L1_CH],
                          (acumT_g, acumT_g.ap[:, cc * 32 + h:cc * 32 + h + 1]),
                          ALU.subtract_rev if hasattr(ALU, "subtract_rev")
                          else ALU.subtract, tag="l1diff")
                if STAGE == 362:
                    continue
                if STAGE == 363:
                    kb.act("S", ctx.sc[0], ctx.sc[0].ap, psb, psb.ap, AF.Exp,
                           tag="l1ebcA")
                    kb.copy("V", expbc, expbc.ap[:, qsl], ctx.sc[0],
                            ctx.sc[0].ap, tag="l1ebcB")
                else:
                    kb.act("S", expbc, expbc.ap[:, qsl], psb, psb.ap, AF.Exp,
                           tag="l1ebc")
            if STAGE < 37:
                continue
            kb.op("V", (lambda v, o=diffs.ap,
                        a=diffs.ap,
                        b=apv(trilm, 0, [[128, 128], [0, 32], [1, 128]]):
                        v.tensor_tensor(o, a, b, ALU.add)),
                  reads=[diffs, trilm], writes=[diffs], tag="l1mka")
            if STAGE < 38:
                continue
            kb.act("S", diffs, diffs.ap, diffs, diffs.ap, AF.Exp, tag="l1mke")
            mall = diffs
            if STAGE < 40:
                continue
            ps = ctx.psum_next()
            kb.mm(ps, ps.ap[:, 0:L1_CH], hbca_g,
                  hbca_g.ap[:, 16 * TCH + cc * L1_CH:16 * TCH + (cc + 1) * L1_CH],
                  hbca_g,
                  hbca_g.ap[:, 17 * TCH + cc * L1_CH:17 * TCH + (cc + 1) * L1_CH],
                  True, True, tag="l1s0")
            kb.copy("V", s0t, s0t.ap, ps, ps.ap[:, 0:L1_CH], tag="l1s0e")
            kb.op("V", (lambda v, o=mall.ap, a=mall.ap,
                        b=apv(s0t, 0, [[L1_CH, 128], [0, 32], [1, 128]]):
                        v.tensor_tensor(o, a, b, ALU.mult)),
                  reads=[mall, s0t], writes=[mall], tag="l1mks")
            kb.op("V", (lambda v, o=expbc.ap, a=expbc.ap,
                        b=apv(hbca_g, 17 * TCH + cc * L1_CH,
                              [[18 * TCH, 128], [0, 32], [1, 128]]):
                        v.tensor_tensor(o, a, b, ALU.mult)),
                  reads=[expbc, hbca_g], writes=[expbc], tag="l1rhs")
            rhsall = expbc
            kb.op("T", (lambda t, o=ctx.tr_ps_bf.ap[0:128, 0:L1_CH],
                        i=hbca_g.ap[:, 16 * TCH + cc * L1_CH:
                                    16 * TCH + (cc + 1) * L1_CH],
                        e=eye128b.ap: t.transpose(o, i, e)),
                  reads=[hbca_g, eye128b], writes=[ctx.tr_ps_bf], tag="l1bmT")
            kb.copy("V", bmtm, bmtm.ap, ctx.tr_ps_bf,
                    ctx.tr_ps_bf.ap[0:128, 0:L1_CH], tag="l1bmTe")
            for fi in range(L1_NI):
                kb.op("T", (lambda t, o=ctx.tr_ps_bf.ap[0:128, 0:L1_CH],
                            i=hbca_g.ap[:, fi * TCH + cc * L1_CH:
                                        fi * TCH + (cc + 1) * L1_CH],
                            e=eye128b.ap: t.transpose(o, i, e)),
                      reads=[hbca_g, eye128b], writes=[ctx.tr_ps_bf], tag="l1hsT")
                kb.copy("V", hstm, hstm.ap[:, fi * 128:(fi + 1) * 128],
                        ctx.tr_ps_bf, ctx.tr_ps_bf.ap[0:128, 0:L1_CH],
                        tag="l1hsTe")
            kb.op("V", (lambda v, o=hstm.ap, a=hstm.ap,
                        b=apv(dtT, cc * 32, [[NCHG * 32, 128], [1, 32], [0, 64]]):
                        v.tensor_tensor(o, a, b, ALU.mult)),
                  reads=[hstm, dtT], writes=[hstm], tag="l1hsdt")
            kb.op("V", (lambda v, o=hstmd.ap, a=hstm.ap,
                        b=apv(decT, 0, [[32, 128], [1, 32], [0, 64]]):
                        v.tensor_tensor(o, a, b, ALU.mult)),
                  reads=[hstm, decT], writes=[hstmd], tag="l1hsdec")
            if STAGE < 50:
                continue
            for h in range(L1_H):
                ps = ctx.psum_next()
                kb.mm(ps, ps.ap[0:64, 0:L1_CH],
                      hstm, hstm.ap[:, h * 64:(h + 1) * 64],
                      mall, mall.ap[:, h * L1_CH:(h + 1) * L1_CH],
                      True, False, tag="l1ymm")
                kb.mm(ps, ps.ap[0:64, 0:L1_CH],
                      S_sb, S_sb.ap[:, h * 64:(h + 1) * 64],
                      rhsall, rhsall.ap[:, h * L1_CH:(h + 1) * L1_CH],
                      False, True, tag="l1yoff")
                fi = h // 2
                hb = (h % 2) * 64
                kb.op("V", (lambda v,
                            o=Y_c.ap[hb:hb + 64,
                                     fi * L1_CH:(fi + 1) * L1_CH],
                            a=hbca_g.ap[hb:hb + 64,
                                        fi * TCH + cc * L1_CH:
                                        fi * TCH + (cc + 1) * L1_CH],
                            s=dcol.ap[hb:hb + 64, fi:fi + 1],
                            b=ps.ap[0:64, 0:L1_CH]:
                            v.scalar_tensor_tensor(o, a, s, b, ALU.mult,
                                                   ALU.add)),
                      reads=[hbca_g, dcol, ps], writes=[Y_c], tag="l1yd")
            for h in range(L1_H):
                ps = ctx.psum_next()
                kb.mm(ps, ps.ap[:, 0:64], bmtm, bmtm.ap, hstmd,
                      hstmd.ap[:, h * 64:(h + 1) * 64], True, True,
                      tag="l1umm")
                kb.op("V", (lambda v, o=S_sb.ap[:, h * 64:(h + 1) * 64],
                            a=S_sb.ap[:, h * 64:(h + 1) * 64],
                            s=dtotb.ap[:, h:h + 1], b=ps.ap[:, 0:64]:
                            v.scalar_tensor_tensor(o, a, s, b, ALU.mult,
                                                   ALU.add)),
                      reads=[S_sb, dtotb, ps], writes=[S_sb], tag="l1sup")
            if STAGE < 60 or c_glob < NCHG:
                continue
            for fi in range(L1_NI):
                fsl = slice(fi * L1_CH, (fi + 1) * L1_CH)
                kb.tt("V", Y_c, Y_c.ap[:, fsl], Y_c, Y_c.ap[:, fsl],
                      gate_g, gate_g.ap[:, fi * TCH + cc * L1_CH:
                                        fi * TCH + (cc + 1) * L1_CH],
                      ALU.mult, tag="l1yf")
                kb.act("S", sqs, sqs.ap, Y_c, Y_c.ap[:, fsl], AF.Square,
                       tag="l1ysq")
                kb.mm(ctx.stats_ps, ctx.stats_ps.ap[:, 0:L1_CH], ctx.onesA,
                      ctx.onesA.ap, sqs, sqs.ap, fi == 0, fi == L1_NI - 1,
                      tag="l1yst")
            RA, RB = ctx.rowsA, ctx.rowsB
            kb.act("S", RA, RA.ap[0:1, 0:L1_CH], ctx.stats_ps,
                   ctx.stats_ps.ap[0:1, 0:L1_CH], AF.Sqrt,
                   bias=(eps_c, eps_c.ap[0:1, :]), scale=1.0 / L1_INTER, tag="l1rms")
            kb.op("V", (lambda v, o=RB.ap[0:1, 0:L1_CH], i=RA.ap[0:1, 0:L1_CH]:
                        v.reciprocal(o, i)), reads=[RA], writes=[RB],
                  tag="l1rmsr")
            kb.mm(ctx.bc_ps[0], ctx.bc_ps[0].ap[:, 0:L1_CH], ctx.ones_col,
                  ctx.ones_col.ap, RB, RB.ap[0:1, 0:L1_CH], True, True,
                  tag="l1rmsb")
            for fi in range(L1_NI):
                fsl = slice(fi * L1_CH, (fi + 1) * L1_CH)
                kb.tt("V", Y_c, Y_c.ap[:, fsl], Y_c, Y_c.ap[:, fsl],
                      ctx.bc_ps[0], ctx.bc_ps[0].ap[:, 0:L1_CH], ALU.mult,
                      tag="l1ynr")
                kb.ts("V", yn_c, yn_c.ap[:, fsl], Y_c, Y_c.ap[:, fsl],
                      (nrmw, nrmw.ap[:, fi:fi + 1]), ALU.mult, tag="l1ynw")
            tr0 = (c_glob - NCHG) * L1_CH
            for oc in range(NT):
                ps = ctx.psum_next()
                for fi in range(L1_NI):
                    kb.mm(ps, ps.ap[:, 0:L1_CH], wout[oc],
                          wout[oc].ap[:, fi * 128:(fi + 1) * 128],
                          yn_c, yn_c.ap[:, fi * L1_CH:(fi + 1) * L1_CH],
                          fi == 0, fi == L1_NI - 1, tag="l1om")
                par = oc % 2
                xr_, xo_ = ctx.xr[par], ctx.xo[par]
                kb.dma("QS", xr_, xr_.ap[:, 0:L1_CH], x_dram,
                       x_dram.ap[oc * 128:(oc + 1) * 128, tr0:tr0 + L1_CH])
                kb.tt("V", xo_, xo_.ap[:, 0:L1_CH], ps, ps.ap[:, 0:L1_CH],
                      xr_, xr_.ap[:, 0:L1_CH], ALU.add, tag="l1res")
                kb.dma("QS", xout_dram,
                       xout_dram.ap[oc * 128:(oc + 1) * 128, tr0:tr0 + L1_CH],
                       xo_, xo_.ap[:, 0:L1_CH])
    if STAGE < 60:
        _l1_dummy_out(kb, ctx, x_dram, xout_dram)


def _l1_dummy_out(kb, ctx, x_dram, xout_dram):
    for oc in range(NT):
        for tc in range(ctx.ntch):
            sl = slice(tc * TCH, (tc + 1) * TCH)
            xr_ = ctx.xr[0]
            kb.dma("QS", xr_, xr_.ap, x_dram,
                   x_dram.ap[oc * 128:(oc + 1) * 128, sl])
            kb.dma("QS", xout_dram,
                   xout_dram.ap[oc * 128:(oc + 1) * 128, sl], xr_, xr_.ap)


def l1_attn_host_inputs(inputs):
    ipw = np.asarray(inputs["L1_attn_in_proj_w"], np.float32)
    conv_w = np.asarray(inputs["L1_attn_conv_w"], np.float32)
    conv_b = np.asarray(inputs["L1_attn_conv_b"], np.float32)
    Dh = np.asarray(inputs["L1_attn_D"], np.float32)
    nw = np.asarray(inputs["L1_attn_norm_w"], np.float32)
    opw = np.asarray(inputs["L1_attn_out_proj_w"], np.float32)
    ipT = np.ascontiguousarray(ipw.T)
    base = {
        "inp": pack_lhsT(np.ascontiguousarray(ipT[:, :4352])),
        "wdtp": np.ascontiguousarray(
            ipT[:, 4352:].reshape(8, 128, 32).transpose(1, 0, 2)
            .reshape(128, 256)),
        "convw": np.ascontiguousarray(
            conv_w[:, 0, :].reshape(18, 128, 4).transpose(1, 0, 2)
            .reshape(128, 72)),
        "convb": np.ascontiguousarray(conv_b.reshape(18, 128).T),
        "dt_bias": np.asarray(inputs["L1_attn_dt_bias"],
                              np.float32).reshape(32, 1),
        "a_log": np.asarray(inputs["L1_attn_A_log"], np.float32).reshape(32, 1),
        "dpack": np.ascontiguousarray(
            np.repeat(Dh, 64).reshape(16, 128).T),
        "nwpack": np.ascontiguousarray(nw.reshape(16, 128).T),
        "outp": pack_lhsT(np.ascontiguousarray(opw.T)),
        "trilm": np.where(np.arange(128)[None, :] >= np.arange(128)[:, None],
                          0.0, -1e4).astype(np.float32),
        "eye128": np.eye(128, dtype=np.float32),
    }
    return base


def l1_attn_halos(x_full):
    xt = x_full.reshape(BATCH * SEQ, D).T
    halos, hmasks = [], []
    for i in range(N_CORES):
        t0 = i * T_CORE
        if t0 % SEQ == 0:
            halos.append(np.zeros((D, L1_HALO), np.float32))
            hmasks.append(np.zeros((128, 1), np.float32))
        else:
            halos.append(np.ascontiguousarray(
                xt[:, t0 - L1_HALO:t0]).astype(np.float32))
            hmasks.append(np.ones((128, 1), np.float32))
    return halos, hmasks


L1A_WSPEC = {"inp": ((34, 128, 1024), F32), "wdtp": ((128, 256), F32),
             "convw": ((128, 72), F32), "convb": ((128, 18), F32),
             "dt_bias": ((32, 1), F32), "a_log": ((32, 1), F32),
             "dpack": ((128, 16), F32), "nwpack": ((128, 16), F32),
             "outp": ((8, 128, 2048), F32),
             "trilm": ((128, 128), F32), "eye128": ((128, 128), F32),
             "x_halo": ((D, L1_HALO), F32), "hmask": ((128, 1), F32)}



_CACHE = {}


def _get(name, builder):
    if name not in _CACHE:
        _CACHE[name] = builder()
    return _CACHE[name]


def _shard_halos(shards, k):
    halos, hmasks = [], []
    for i in range(N_CORES):
        if (i * T_CORE) % SEQ == 0:
            halos.append(np.zeros((D, k), np.float32))
            hmasks.append(np.zeros((128, 1), np.float32))
        else:
            halos.append(np.ascontiguousarray(shards[i - 1][:, T_CORE - k:]))
            hmasks.append(np.ones((128, 1), np.float32))
    return halos, hmasks


def kernel(**inputs):
    CORES = list(range(N_CORES))
    x = np.ascontiguousarray(np.asarray(inputs["L0_attn_u"], np.float32))
    shards = shard_tokens(x)

    base0, offsels, pmasks = _get("l0h", lambda: l0_attn_host_inputs(inputs))
    nc = _get("l0a", build_l0a_launch)
    wa = {k: base0[k] for k in L0A_WSPEC if k != "offsel"}
    res = run_bass_kernel_spmd(
        nc, [dict(x_in=shards[i], offsel=offsels[i], **wa) for i in CORES],
        core_ids=CORES)
    ys = [np.asarray(r["y_out"]) for r in res.results]
    tots = np.stack([np.asarray(r["tot_out"]) for r in res.results])
    nc = _get("l0b", build_l0b_launch)
    wb = {k: base0[k] for k in L0B_WSPEC if k not in ("offsel", "pmask")}
    res = run_bass_kernel_spmd(
        nc, [dict(x_in=shards[i], y_in=ys[i], tots=tots, offsel=offsels[i],
                  pmask=pmasks[i], **wb) for i in CORES], core_ids=CORES)
    shards = [np.asarray(r["x_out"]) for r in res.results]

    w = _get("l0fh", lambda: l0_ffn_host_inputs(inputs))
    nc = _get("l0f", lambda: build_block_launch(emit_l0_ffn, L0_FFN_WSPEC))
    res = run_bass_kernel_spmd(
        nc, [dict(x_in=shards[i], **w) for i in CORES], core_ids=CORES)
    shards = [np.asarray(r["x_out"]) for r in res.results]

    w = _get("l1ah", lambda: l1_attn_host_inputs(inputs))
    halos, hmasks = _shard_halos(shards, L1_HALO)
    nc = _get("l1a", lambda: build_block_launch(emit_l1_attn, L1A_WSPEC))
    res = run_bass_kernel_spmd(
        nc, [dict(x_in=shards[i], x_halo=halos[i], hmask=hmasks[i], **w)
             for i in CORES], core_ids=CORES)
    shards = [np.asarray(r["x_out"]) for r in res.results]

    w = _get("l1fh", lambda: l1_ffn_host_inputs(inputs))
    halos, _ = _shard_halos(shards, 1)
    nc = _get("l1f", lambda: build_block_launch(emit_l1_ffn, L1_FFN_WSPEC))
    res = run_bass_kernel_spmd(
        nc, [dict(x_in=shards[i], x_halo=halos[i], **w) for i in CORES],
        core_ids=CORES)
    shards = [np.asarray(r["x_out"]) for r in res.results]

    w2, sdecays = _get("l2ah", lambda: l2_attn_host_inputs(inputs))
    nc = _get("l2a", build_l2a_launch)
    wa = {k: w2[k] for k in L2A_WSPEC}
    res = run_bass_kernel_spmd(
        nc, [dict(x_in=shards[i], **wa) for i in CORES], core_ids=CORES)
    rets = [np.asarray(r["ret_out"]) for r in res.results]
    slocals = np.stack([np.asarray(r["s_out"]) for r in res.results])
    nc = _get("l2b", build_l2b_launch)
    wb = {k: w2[k] for k in L2B_WSPEC if k != "sdecay"}
    res = run_bass_kernel_spmd(
        nc, [dict(x_in=shards[i], ret_in=rets[i], spre=slocals,
                  sdecay=sdecays[i], **wb) for i in CORES], core_ids=CORES)
    shards = [np.asarray(r["x_out"]) for r in res.results]

    w = _get("l2fh", lambda: l2_ffn_host_inputs(inputs))
    nc = _get("l2f", lambda: build_block_launch(emit_l2_ffn, L2_FFN_WSPEC))
    res = run_bass_kernel_spmd(
        nc, [dict(x_in=shards[i], **w) for i in CORES], core_ids=CORES)
    shards = [np.asarray(r["x_out"]) for r in res.results]

    w = _get("l3ah", lambda: l3_attn_host_inputs(inputs))
    halos, hmasks = _shard_halos(shards, L3A_HALO)
    nc = _get("l3a", lambda: build_block_launch(emit_l3_attn, L3A_WSPEC))
    res = run_bass_kernel_spmd(
        nc, [dict(x_in=shards[i], x_halo=halos[i], hmask=hmasks[i], **w)
             for i in CORES], core_ids=CORES)
    shards = [np.asarray(r["x_out"]) for r in res.results]

    w = _get("l3fh", lambda: l3_ffn_host_inputs(inputs))
    nc = _get("l3f", lambda: build_block_launch(emit_l3_ffn, L3_FFN_WSPEC))
    res = run_bass_kernel_spmd(
        nc, [dict(x_in=shards[i], **w) for i in CORES], core_ids=CORES)
    shards = [np.asarray(r["x_out"]) for r in res.results]

    return unshard_tokens(shards).astype(np.float32)
